# version 08 (final)
# speedup vs baseline: 1.4521x; 1.1087x over previous
import math
import numpy as np
from contextlib import ExitStack

import concourse.bass as bass
import concourse.mybir as mybir
from concourse.bass_utils import run_bass_kernel_spmd

F32 = mybir.dt.float32
BF16 = mybir.dt.bfloat16
AF = mybir.ActivationFunctionType
ALU = mybir.AluOpType
AX = mybir.AxisListType

T_TOK = 4096
D = 1024
NEG = -1.0e30
ENGINES = ("tensor", "vector", "scalar", "gpsimd", "sync")

T_SCHED = [64, 64, 128, 128, 256, 256] + [512] * 10


def _slopes64():
    h = np.arange(1, 17, dtype=np.float64)
    return (2.0 ** (-8.0 / 16.0)) ** h



class Sem:
    def __init__(self, ctx, nc, name):
        self.h = nc.alloc_semaphore(name=name)
        self.n = 0

    def inc(self, k=1):
        self.n += k
        return self.n


SELFCHAIN = ("vector", "scalar")


class Prog:
    def __init__(self, nc, ctx):
        self.nc = nc
        self.ctx = ctx
        self.q = {e: [] for e in ENGINES}
        self.tags = {e: [] for e in ENGINES}
        self.tag = "main"
        self.lastwait = {e: {} for e in ENGINES}
        self.bar = Sem(ctx, nc, "bar")
        self.selfsem = {e: Sem(ctx, nc, f"self_{e}") for e in SELFCHAIN}

    def _push(self, e, fn):
        self.q[e].append(fn)
        self.tags[e].append(self.tag)

    def wait(self, e, sem, v):
        if v is None or v <= 0:
            return
        k = id(sem)
        if self.lastwait[e].get(k, -1) >= v:
            return
        self.lastwait[e][k] = v
        self._push(e, lambda eng, s=sem, v=v: eng.wait_ge(s.h, v))

    def op(self, e, fn, waits=(), inc=None, inc_n=1, chain=True):
        for (s, v) in waits:
            self.wait(e, s, v)
        if e in SELFCHAIN and chain:
            prevedge = getattr(self, "lastdone", {}).get(e)
            if not hasattr(self, "lastdone"):
                self.lastdone = {}
            if inc is not None:
                tgt = inc.inc(inc_n)
                self.lastdone[e] = (inc, tgt)
            else:
                sc = self.selfsem[e]
                tgt = sc.inc(1)
                self.lastdone[e] = (sc, tgt)
                inc = sc
                inc_n = 1

            def emitfn(eng, fn=fn, prevedge=prevedge, inc=inc, inc_n=inc_n):
                ins = fn(eng)
                if prevedge is not None and prevedge[1] > 0:
                    ins._wait_ge(prevedge[0].h, prevedge[1])
                ins.then_inc(inc.h, inc_n)
            self._push(e, emitfn)
            return tgt if inc is not None else None
        if inc is not None:
            tgt = inc.inc(inc_n)
            self._push(e, lambda eng, fn=fn, s=inc, n=inc_n: fn(eng).then_inc(s.h, n))
            return tgt
        self._push(e, lambda eng, fn=fn: fn(eng))
        return None

    def transpose_l(self, apfn, sem, waits=()):
        def fn(e, apfn=apfn):
            return e.transpose(*apfn(e))
        v = self.op("tensor", fn, waits=waits, inc=sem)
        self.wait("tensor", sem, v)
        return v

    def barrier(self, engines=ENGINES):
        for e in engines:
            if e in SELFCHAIN:
                le = getattr(self, "lastdone", {}).get(e)
                if le is not None:
                    self.wait(e, le[0], le[1])
        tgt = self.bar.n + len(engines)
        self.bar.n = tgt
        for e in engines:
            self._push(e, lambda eng, s=self.bar: eng.nop().then_inc(s.h, 1))
        for e in engines:
            self.lastwait[e][id(self.bar)] = tgt
            self._push(e, lambda eng, s=self.bar, v=tgt: eng.wait_ge(s.h, v))

    def run(self):
        with self.nc.Block() as blk:
            def mk(e):
                def body(eng):
                    cur = None
                    cm = None
                    for fn, tg in zip(self.q[e], self.tags[e]):
                        if tg != cur:
                            if cm is not None:
                                cm.__exit__(None, None, None)
                            cm = self.nc.named_scope(tg)
                            cm.__enter__()
                            cur = tg
                        fn(eng)
                    if cm is not None:
                        cm.__exit__(None, None, None)
                return body
            blk.tensor(mk("tensor"))
            blk.vector(mk("vector"))
            blk.scalar(mk("scalar"))
            blk.gpsimd(mk("gpsimd"))
            blk.sync(mk("sync"))


def sbuf(ctx, nc, name, shape, dtype):
    return ctx.enter_context(nc.sbuf_tensor(name, shape, dtype))


def psum(ctx, nc, name, shape, dtype=F32):
    return ctx.enter_context(nc.psum_tensor(name, shape, dtype))



class LNKit:

    def __init__(self, ctx, nc, P, consts, W=512, tag="ln", ones_col=None, ones_row=None, **kw):
        self.P, self.W = P, W
        self.sq = sbuf(ctx, nc, f"{tag}_sq", [128, W], F32)
        self.tmp = sbuf(ctx, nc, f"{tag}_tmp", [128, W], F32)
        self.r_m = sbuf(ctx, nc, f"{tag}_rm", [1, W], F32)
        self.r_r = sbuf(ctx, nc, f"{tag}_rr", [1, W], F32)
        self.r_t = sbuf(ctx, nc, f"{tag}_rt", [1, W], F32)
        if "psums" in kw and kw["psums"] is not None:
            self.p_stats, self.p_mb, self.p_rb = kw["psums"]
        else:
            self.p_stats = psum(ctx, nc, f"{tag}_pst", [33, W])
            self.p_mb = psum(ctx, nc, f"{tag}_pmb", [128, W])
            self.p_rb = psum(ctx, nc, f"{tag}_prb", [128, W])
        self.s_sq = Sem(ctx, nc, f"{tag}_ssq")
        self.s_st = Sem(ctx, nc, f"{tag}_sst")
        self.s_row = Sem(ctx, nc, f"{tag}_srow")
        self.s_bc = Sem(ctx, nc, f"{tag}_sbc")
        self.s_ln = Sem(ctx, nc, f"{tag}_sln")
        self.s_free = Sem(ctx, nc, f"{tag}_sfree")
        self.ones_col = ones_col
        self.ones_row = ones_row
        self.eps_ap = kw.get("eps_ap")

    def emit(self, xch, lnch, xwaits, eps=1e-5):
        P, W = self.P, self.W
        oc = self.ones_col[:, :]
        orow = self.ones_row[:, :]
        sq0 = self.s_sq.n
        st0 = self.s_st.n
        free0 = self.s_free.n
        for c in range(8):
            P.op("scalar", lambda e, c=c: e.activation(self.sq[:, :], xch[c], AF.Square),
                 waits=[xwaits[c], (self.s_st, st0 + c), (self.s_free, free0)],
                 inc=self.s_sq)
            P.op("tensor", lambda e, c=c: e.matmul(self.p_stats[32:33, :], oc, self.sq[:, :],
                                                   start=(c == 0), stop=(c == 7)),
                 waits=[(self.s_sq, sq0 + c + 1), (self.s_free, free0)], inc=self.s_st)
        for c in range(8):
            P.op("tensor", lambda e, c=c: e.matmul(self.p_stats[0:1, :], oc, xch[c],
                                                   start=(c == 0), stop=(c == 7)),
                 waits=[xwaits[c]], inc=(self.s_st if c == 7 else None))
        st_v = self.s_st.n
        P.op("vector", lambda e: e.tensor_scalar(self.r_m[:, :], self.p_stats[0:1, :],
                                                 1.0 / 1024.0, None, ALU.mult),
             waits=[(self.s_st, st_v)])
        P.op("vector", lambda e: e.tensor_mul(self.r_t[:, :], self.r_m[:, :], self.r_m[:, :]))
        P.op("vector", lambda e: e.scalar_tensor_tensor(self.r_t[:, :], self.p_stats[32:33, :],
                                                        1.0 / 1024.0, self.r_t[:, :],
                                                        ALU.mult, ALU.subtract),
             inc=self.s_free)
        P.op("scalar", lambda e, eps=eps: e.activation(self.r_t[:, :], self.r_t[:, :], AF.Ln, bias=np.float32(eps)),
             waits=[(self.s_free, self.s_free.n)])
        P.op("scalar", lambda e: e.activation(self.r_r[:, :], self.r_t[:, :], AF.Exp, scale=np.float32(-0.5)),
             inc=self.s_row)
        row_v = self.s_row.n
        P.op("tensor", lambda e: e.matmul(self.p_mb[:, :], orow, self.r_m[:, :], start=True, stop=True),
             waits=[(self.s_row, row_v)])
        P.op("tensor", lambda e: e.matmul(self.p_rb[:, :], orow, self.r_r[:, :], start=True, stop=True),
             inc=self.s_bc)
        bc_v = self.s_bc.n
        for c in range(8):
            P.op("vector", lambda e, c=c: e.tensor_sub(self.tmp[:, :], xch[c], self.p_mb[:, :]),
                 waits=[(self.s_bc, bc_v), (self.s_ln, self.s_ln.n)])
            P.op("vector", lambda e, c=c: e.tensor_mul(lnch[c], self.tmp[:, :], self.p_rb[:, :]),
                 inc=self.s_ln)
        return (self.s_ln, self.s_ln.n)



def make_consts(nc):
    C = {}
    ident = np.eye(128, dtype=np.float32)
    C["identity"] = nc.inline_tensor(ident, "c_ident")
    C["ones128x1"] = nc.inline_tensor(np.ones((128, 1), np.float32), "c_o128")
    C["ones1x128"] = nc.inline_tensor(np.ones((1, 128), np.float32), "c_o1x")
    C["ones1x64"] = nc.inline_tensor(np.ones((1, 64), np.float32), "c_o1x64")
    posc = (np.arange(128)[:, None] + 128 * np.arange(32)[None, :]).astype(np.float32)
    C["posc"] = nc.inline_tensor(posc, "c_posc")
    hsc = np.zeros((128, 8 * 16), np.float32)
    for c in range(8):
        for p in range(128):
            hsc[p, c * 16 + 2 * c + p // 64] = 1.0
    import ml_dtypes as _mld
    C["hsc"] = nc.inline_tensor(hsc.astype(_mld.bfloat16), "c_hsc")
    import ml_dtypes
    C["ident_bf"] = nc.inline_tensor(np.eye(128).astype(ml_dtypes.bfloat16), "c_identb")
    C["i8"] = nc.inline_tensor(np.eye(8).astype(ml_dtypes.bfloat16), "c_i8")
    C["ones8x128"] = nc.inline_tensor(np.ones((8, 128)).astype(ml_dtypes.bfloat16), "c_o8x")
    C["epsv"] = nc.inline_tensor(np.array([[1e-5, 1e-6, 0.0, 1.0]], np.float32), "c_epsv")
    C["selb16"] = nc.inline_tensor(
        np.kron(np.eye(16), np.ones((1, 128))).astype(ml_dtypes.bfloat16), "c_selb16")
    mk = np.zeros((32, 16), np.float32)
    for k in range(32):
        mk[k, k // 2] = 1.0
    C["mask16"] = nc.inline_tensor(mk, "c_mask16")
    pc = np.zeros((32, 128), np.float32)
    for k in range(32):
        for r in range(128):
            if k % 2 == r // 64:
                pc[k, r] = 1.0
    C["parc"] = nc.inline_tensor(pc, "c_parc")
    C["tri_incl"] = nc.inline_tensor(np.triu(np.ones((128, 128), np.float32)), "c_tri")
    C["tri_le_bf"] = nc.inline_tensor(
        np.triu(np.ones((128, 128))).astype(ml_dtypes.bfloat16), "c_trile")
    C["i32"] = nc.inline_tensor(np.eye(32, dtype=np.float32), "c_i32")
    C["ones32x128b"] = nc.inline_tensor(np.ones((32, 128)).astype(ml_dtypes.bfloat16), "c_o32b")
    C["ones32x128f"] = nc.inline_tensor(np.ones((32, 128), np.float32), "c_o32f")
    m128 = np.where(np.arange(128)[:, None] > np.arange(128)[None, :],
                    np.float32(NEG), np.float32(0.0)).astype(np.float32)
    C["mask128"] = nc.inline_tensor(m128, "c_mask128")
    slopes = _slopes64()
    cols, colmap = [], {}
    for h in range(16):
        T = T_SCHED[h]
        for tb in range(T_TOK // T):
            colmap[(h, tb)] = len(cols)
            cols.append(np.float32(slopes[h] * (tb * T + (T - 1))))
    trefc = np.broadcast_to(np.asarray(cols, np.float32)[None, :],
                            (128, len(cols))).copy()
    C["trefc"] = nc.inline_tensor(trefc, "c_trefc")
    C["trefmap"] = colmap
    C["slopes"] = slopes
    return C



def emit_l0_attn(nc, ctx, P, C, hbm):
    W = 512
    NT = T_TOK // W
    xT = hbm["xT"].ap()
    x1T = hbm["x1T"].ap()
    wnames = ("Wq", "Wk", "Wv", "Wo")
    wap = {n: hbm[n].ap() for n in wnames}
    bqv = hbm["bq"].ap()
    bkv = hbm["bk"].ap()
    bvv = hbm["bv"].ap()
    bov = hbm["bo"].ap()

    wb = [sbuf(ctx, nc, "a_wb0", [128, 8 * 1024], BF16)]
    wb.append(wb[0])
    bq_s = sbuf(ctx, nc, "a_bq", [128, 8], F32)
    bk_s = sbuf(ctx, nc, "a_bk", [128, 8], F32)
    bo_s = sbuf(ctx, nc, "a_bo", [128, 8], F32)
    bvb = sbuf(ctx, nc, "a_bvb", [128, 1024], BF16)

    xt = [sbuf(ctx, nc, f"a_x{c}", [128, W], F32) for c in range(8)]
    lnx = [sbuf(ctx, nc, f"a_ln{c}", [128, W], BF16) for c in range(8)]
    QTt = sbuf(ctx, nc, "a_QTt", [128, 8 * W], BF16)
    KT = sbuf(ctx, nc, "a_KT", [128, 8 * T_TOK], BF16)
    VS = sbuf(ctx, nc, "a_VS", [128, 32 * 16 * 65], BF16)
    ATTt = sbuf(ctx, nc, "a_ATTt", [128, 8 * W], BF16)
    prod = sbuf(ctx, nc, "a_prod", [128, W], BF16)
    MADJ = sbuf(ctx, nc, "a_MADJ", [128, 64], F32)
    biasc = sbuf(ctx, nc, "a_biasc", [128, 2 * 32], F32)
    pT = [sbuf(ctx, nc, f"a_pT{i}", [128, 512], BF16) for i in range(2)]
    rrow = sbuf(ctx, nc, "a_rrow", [1, 512], F32)
    rb = sbuf(ctx, nc, "a_rb", [64, 512], F32)
    otmp = sbuf(ctx, nc, "a_otmp", [64, 512], BF16)
    bmt = sbuf(ctx, nc, "a_bmt", [16, 8], BF16)
    maskc = sbuf(ctx, nc, "a_maskc", [128, 128], BF16)
    posc_s = sbuf(ctx, nc, "a_posc", [128, 32], F32)
    hsc_s = sbuf(ctx, nc, "a_hsc", [128, 128], BF16)
    id128 = sbuf(ctx, nc, "a_id128", [128, 128], BF16)
    i8_s = sbuf(ctx, nc, "a_i8", [8, 8], BF16)
    o8x128 = sbuf(ctx, nc, "a_o8x", [8, 128], BF16)
    bmtT = sbuf(ctx, nc, "a_bmtT", [8, 16], F32)
    dg8 = sbuf(ctx, nc, "a_dg8", [8, 8], BF16)
    o1x128 = sbuf(ctx, nc, "a_o1x", [1, 128], F32)
    o1x64 = sbuf(ctx, nc, "a_o1x64", [1, 64], F32)
    o128x1 = sbuf(ctx, nc, "a_o128", [128, 1], F32)
    epsv_s = sbuf(ctx, nc, "a_eps", [1, 4], F32)

    p_qkv = psum(ctx, nc, "a_pqkv", [128, W])
    p_sc = [psum(ctx, nc, f"a_psc{i}", [128, 512]) for i in range(2)]
    p_av = psum(ctx, nc, "a_pav", [65, 512])
    p_bcr = psum(ctx, nc, "a_pbcr", [64, 512])
    p_dg = psum(ctx, nc, "a_pdg", [16, 512])
    p_mb = psum(ctx, nc, "a_pmbh", [128, 8])
    p_bt = psum(ctx, nc, "a_pbt", [8, 16], BF16)

    s_w = Sem(ctx, nc, "a_sw")
    s_wg = Sem(ctx, nc, "a_swg")
    s_x = Sem(ctx, nc, "a_sx")
    s_pq = Sem(ctx, nc, "a_spq")
    s_ev = Sem(ctx, nc, "a_sev")
    s_dg = Sem(ctx, nc, "a_sdg")
    s_sc = Sem(ctx, nc, "a_ssc")
    s_msk = Sem(ctx, nc, "a_smsk")
    s_exp = Sem(ctx, nc, "a_sexp")
    s_av = Sem(ctx, nc, "a_sav")
    s_nrm = Sem(ctx, nc, "a_snrm")
    s_sh = Sem(ctx, nc, "a_ssh")
    s_y = Sem(ctx, nc, "a_sy")
    s_st = Sem(ctx, nc, "a_sst")
    s_bias = Sem(ctx, nc, "a_sbias")

    ln = LNKit(ctx, nc, P, {"ones128x1": None}, W=W, tag="a_ln",
               ones_col=o128x1, ones_row=o1x128,
               psums=(p_sc[0], p_sc[1], p_qkv), eps_ap=epsv_s[0:1, 0:1])

    slopes = C["slopes"]

    P.op("sync", lambda e: e.dma_start(out=o1x128[:, :], in_=C["ones1x128"].ap()[:, :]), inc=s_w, inc_n=16)
    P.op("sync", lambda e: e.dma_start(out=o1x64[:, :], in_=C["ones1x64"].ap()[:, :]), inc=s_w, inc_n=16)
    P.op("sync", lambda e: e.dma_start(out=o128x1[:, :], in_=C["ones128x1"].ap()[:, :]), inc=s_w, inc_n=16)
    P.op("sync", lambda e: e.dma_start(out=epsv_s[:, :], in_=C["epsv"].ap()[:, :]), inc=s_w, inc_n=16)
    P.op("sync", lambda e: e.dma_start(out=posc_s[:, :], in_=C["posc"].ap()[:, :]), inc=s_w, inc_n=16)
    P.op("sync", lambda e: e.dma_start(out=hsc_s[:, :], in_=C["hsc"].ap()[:, :]), inc=s_w, inc_n=16)
    P.op("gpsimd", lambda e: e.dma_start(out=maskc[:, :], in_=C["mask128"].ap()[:, :]), inc=s_wg, inc_n=16)
    P.op("gpsimd", lambda e: e.dma_start(out=id128[:, :], in_=C["ident_bf"].ap()[:, :]), inc=s_wg, inc_n=16)
    P.op("gpsimd", lambda e: e.dma_start(out=i8_s[:, :], in_=C["i8"].ap()[:, :]), inc=s_wg, inc_n=16)
    P.op("gpsimd", lambda e: e.dma_start(out=o8x128[:, :], in_=C["ones8x128"].ap()[:, :]), inc=s_wg, inc_n=16)
    P.op("sync", lambda e: e.dma_start(out=bq_s[:, :], in_=bqv), inc=s_w, inc_n=16)
    P.op("sync", lambda e: e.dma_start(out=bk_s[:, :], in_=bkv), inc=s_w, inc_n=16)
    P.op("sync", lambda e: e.dma_start(out=bo_s[:, :], in_=bov), inc=s_w, inc_n=16)
    wv_ready = s_w.n
    for half in range(2):
        v = P.op("sync", lambda e, half=half: e.dma_start(
            out=rrow[:, :], in_=bvv[:, half * 512:(half + 1) * 512]),
            waits=[(s_ev, s_ev.n)], inc=s_w, inc_n=16)
        P.op("tensor", lambda e: e.matmul(p_sc[0][:, :], o1x128[:, :],
                                          rrow[:, :], start=True, stop=True),
             waits=[(s_w, v)], inc=s_pq)
        P.op("vector", lambda e, half=half: e.tensor_copy(bvb[:, half * 512:(half + 1) * 512],
                                                          p_sc[0][:, :]),
             waits=[(s_pq, s_pq.n)], inc=s_ev)
        P.wait("sync", s_ev, s_ev.n)
    P.op("vector", lambda e: e.memset(VS[:, :], 1.0), inc=s_ev)
    P.wait("vector", s_wg, s_wg.n)
    P.wait("tensor", s_wg, s_wg.n)
    P.barrier()

    def wload(i, name, waits=()):
        sap = wap[name].rearrange("(a p) d -> p a d", p=128)
        return P.op("gpsimd", lambda e, i=i, sap=sap: e.dma_start(
            out=wb[i][:, :].rearrange("p (a d) -> p a d", a=8), in_=sap),
            waits=waits, inc=s_wg, inc_n=16)

    trefmap_local = {}
    colv = []
    for h in range(16):
        for tbj in range(W // T_SCHED[h]):
            trefmap_local[(h, tbj)] = len(colv)
            colv.append((h, tbj))
    n_lcol = len(colv)

    for j in range(NT):
        wql = wload(0, "Wq", waits=[(s_pq, s_pq.n)])
        for c in range(8):
            v = P.op("sync", lambda e, c=c, j=j: e.dma_start(
                out=xt[c][:, :], in_=xT[c * 128:(c + 1) * 128, j * W:(j + 1) * W]),
                inc=s_x, inc_n=16)
        xw = [(s_x, s_x.n)] * 8
        P.wait("vector", s_st, s_st.n)
        ln.emit([xt[c][:, :] for c in range(8)],
                [lnx[c][:, :] for c in range(8)], xw)
        P.barrier()
        for dc in range(8):
            for c in range(8):
                P.op("tensor", lambda e, dc=dc, c=c: e.matmul(
                    p_qkv[:, :],
                    wb[0][:, c * 1024 + dc * 128: c * 1024 + (dc + 1) * 128],
                    lnx[c][:, :], start=(c == 0), stop=(c == 7)),
                    waits=[(s_wg, wql)] if c == 0 else (), inc=(s_pq if c == 7 else None))
            P.op("scalar", lambda e, dc=dc: e.activation(
                QTt[:, dc * W:(dc + 1) * W], p_qkv[:, :], AF.Identity,
                bias=bq_s[:, dc:dc + 1]),
                waits=[(s_pq, s_pq.n)], inc=s_ev)
            P.wait("tensor", s_ev, s_ev.n)
        wkl = wload(1, "Wk", waits=[(s_pq, s_pq.n)])
        P.barrier()
        wvl_unused = None
        for dc in range(8):
            for c in range(8):
                P.op("tensor", lambda e, dc=dc, c=c: e.matmul(
                    p_qkv[:, :],
                    wb[1][:, c * 1024 + dc * 128: c * 1024 + (dc + 1) * 128],
                    lnx[c][:, :], start=(c == 0), stop=(c == 7)),
                    waits=[(s_wg, wkl)] if c == 0 else (), inc=(s_pq if c == 7 else None))
            P.op("scalar", lambda e, dc=dc, j=j: e.activation(
                KT[:, dc * T_TOK + j * W: dc * T_TOK + (j + 1) * W], p_qkv[:, :],
                AF.Identity, bias=bk_s[:, dc:dc + 1]),
                waits=[(s_pq, s_pq.n)], inc=s_ev)
            P.wait("tensor", s_ev, s_ev.n)
        wvl = wload(0, "Wv", waits=[(s_pq, s_pq.n)])
        P.barrier()
        for ss in range(4):
            sc = j * 4 + ss
            for dh in range(2):
                for c in range(8):
                    P.op("tensor", lambda e, ss=ss, dh=dh, c=c: e.matmul(
                        p_qkv[:, :],
                        lnx[c][:, ss * 128:(ss + 1) * 128],
                        wb[0][:, c * 1024 + dh * 512: c * 1024 + (dh + 1) * 512],
                        start=(c == 0), stop=(c == 7)),
                        waits=[(s_wg, wvl)] if c == 0 else (), inc=(s_pq if c == 7 else None))
                for hh in range(8):
                    h = dh * 8 + hh
                    off = (sc * 16 + h) * 65
                    P.op("vector", lambda e, hh=hh, off=off, dh=dh: e.tensor_add(
                        VS[:, off:off + 64],
                        p_qkv[:, hh * 64:(hh + 1) * 64],
                        bvb[:, dh * 512 + hh * 64: dh * 512 + (hh + 1) * 64]),
                        waits=[(s_pq, s_pq.n)],
                        inc=(s_ev if hh == 7 else None))
                P.wait("tensor", s_ev, s_ev.n)
        wol = wload(1, "Wo", waits=[(s_pq, s_pq.n)])
        P.barrier()
        for c in range(8):
            P.op("vector", lambda e, c=c, j=j: e.tensor_mul(
                prod[:, :], QTt[:, c * W:(c + 1) * W],
                KT[:, c * T_TOK + j * W: c * T_TOK + (j + 1) * W]),
                inc=s_dg)
            P.op("tensor", lambda e, c=c: e.matmul(
                p_dg[:, :], hsc_s[:, c * 16:(c + 1) * 16], prod[:, :],
                start=(c == 0), stop=(c == 7)),
                waits=[(s_dg, s_dg.n)], inc=s_dg)
            P.wait("vector", s_dg, s_dg.n)
        P.op("vector", lambda e: e.tensor_reduce(
            prod[0:16, 0:8], p_dg[:, :].rearrange("h (b f) -> h b f", b=8),
            AX.X, ALU.max),
            waits=[(s_dg, s_dg.n)], inc=s_dg)
        P.op("vector", lambda e: e.tensor_scalar(bmt[:, :], prod[0:16, 0:8],
                                                 0.125, None, ALU.mult),
             inc=s_dg)
        P.barrier()
        P.transpose_l(lambda e: (p_bt[:, :], bmt[:, :], id128[0:16, 0:16]), s_pq, waits=())
        P.op("vector", lambda e: e.tensor_copy(bmtT[:, :], p_bt[:, :]),
             waits=[(s_pq, s_pq.n)], inc=s_dg)
        for h in range(16):
            T = T_SCHED[h]
            nl = W // T
            f = T // 64
            P.op("vector", lambda e, h=h: e.tensor_scalar(
                dg8[:, :], i8_s[:, :], bmtT[:, h:h + 1], None, ALU.mult),
                waits=[(s_dg, s_dg.n)], inc=s_dg)
            P.op("tensor", lambda e, h=h: e.matmul(
                p_mb[:, :], o8x128[:, :], dg8[:, :],
                start=True, stop=True),
                waits=[(s_dg, s_dg.n)], inc=s_pq)
            base = trefmap_local[(h, 0)]
            if f == 1:
                P.op("vector", lambda e, base=base, nl=nl: e.tensor_copy(
                    MADJ[:, base:base + nl], p_mb[:, :]),
                    waits=[(s_pq, s_pq.n)], inc=s_ev)
            else:
                P.op("vector", lambda e, base=base, nl=nl, f=f: e.tensor_reduce(
                    MADJ[:, base:base + nl],
                    p_mb[:, :].rearrange("p (b f) -> p b f", b=nl),
                    AX.X, ALU.max),
                    waits=[(s_pq, s_pq.n)], inc=s_ev)
            for tbj in range(nl):
                t_ref = j * W + tbj * T + (T - 1)
                P.op("vector", lambda e, base=base, tbj=tbj, h=h, t_ref=t_ref: e.tensor_scalar(
                    MADJ[:, base + tbj: base + tbj + 1],
                    MADJ[:, base + tbj: base + tbj + 1],
                    float(slopes[h] * t_ref), None, ALU.add),
                    inc=(s_ev if tbj == nl - 1 else None))
            P.wait("tensor", s_ev, s_ev.n)
        P.barrier()
        for hp in range(8):
            h0 = 2 * hp
            T = T_SCHED[h0]
            nl = W // T
            for tbj in range(nl):
                for h in (h0, h0 + 1):
                    tb_start = j * W + tbj * T
                    tb_end = tb_start + T - 1
                    n_sc = tb_end // 128 + 1
                    mcol = trefmap_local[(h, tbj)]
                    slot = h % 2
                    P.op("vector", lambda e, slot=slot, n_sc=n_sc, h=h, mcol=mcol: e.tensor_scalar(
                        biasc[:, slot * 32: slot * 32 + n_sc],
                        posc_s[:, 0:n_sc], float(slopes[h]),
                        MADJ[:, mcol:mcol + 1], ALU.mult, ALU.subtract),
                        waits=[(s_av, s_av.n)], inc=s_bias)
                    bias_v = s_bias.n
                    avw = []
                    for sc in range(n_sc):
                        pp = sc % 2
                        chunk = h // 2
                        half = (h % 2) * 64
                        delta = sc * 128 - tb_start
                        lo = max(delta, 0)
                        P.op("tensor", lambda e, pp=pp, chunk=chunk, half=half, sc=sc, tbj=tbj, T=T, lo=lo: e.matmul(
                            p_sc[pp][:, lo:T],
                            KT[half:half + 64, chunk * T_TOK + sc * 128: chunk * T_TOK + (sc + 1) * 128],
                            QTt[half:half + 64, chunk * W + tbj * T + lo: chunk * W + tbj * T + T],
                            start=True, stop=True),
                            waits=[(s_exp, s_exp.n if sc == 0 else s_exp.n - 1)],
                            inc=s_sc)
                        sc_v = s_sc.n
                        expw = [(s_sc, sc_v), (s_bias, bias_v), (s_av, s_av.n - 1)]
                        if delta >= -64:
                            moff = -delta if delta < 0 else 0
                            w = min(128 - moff, T - lo)
                            P.op("vector", lambda e, pp=pp, moff=moff, lo=lo, w=w: e.tensor_add(
                                p_sc[pp][:, lo:lo + w], p_sc[pp][:, lo:lo + w],
                                maskc[:, moff:moff + w]),
                                waits=[(s_sc, sc_v)], inc=s_msk)
                            expw.append((s_msk, s_msk.n))
                        P.op("scalar", lambda e, pp=pp, slot=slot, sc=sc, T=T, lo=lo: e.activation(
                            pT[pp][:, lo:T], p_sc[pp][:, lo:T], AF.Exp,
                            bias=biasc[:, slot * 32 + sc: slot * 32 + sc + 1],
                            scale=np.float32(0.125)),
                            waits=expw, inc=s_exp)
                        exp_v = s_exp.n
                        voff = (sc * 16 + h) * 65
                        v = P.op("tensor", lambda e, pp=pp, voff=voff, sc=sc, n_sc=n_sc, T=T, lo=lo: e.matmul(
                            p_av[:, lo:T], VS[:, voff:voff + 65], pT[pp][:, lo:T],
                            start=(sc == 0), stop=(sc == n_sc - 1)),
                            waits=[(s_exp, exp_v)], inc=s_av)
                        avw.append((s_av, v))
                    P.op("vector", lambda e, T=T: e.reciprocal(rrow[:, 0:T], p_av[64:65, 0:T]),
                         waits=[avw[-1]], inc=s_nrm)
                    P.op("tensor", lambda e, T=T: e.matmul(
                        p_bcr[:, 0:T], o1x64[:, :], rrow[:, 0:T],
                        start=True, stop=True),
                        waits=[(s_nrm, s_nrm.n)], inc=s_nrm)
                    P.op("scalar", lambda e, T=T: e.activation(rb[:, 0:T], p_bcr[:, 0:T], AF.Copy),
                         waits=[(s_nrm, s_nrm.n)], inc=s_nrm)
                    chunk = h // 2
                    loc = chunk * W + tbj * T
                    if h % 2 == 0:
                        P.op("vector", lambda e, loc=loc, T=T: e.tensor_mul(
                            ATTt[0:64, loc:loc + T], p_av[0:64, 0:T], rb[0:64, 0:T]),
                            waits=[(s_nrm, s_nrm.n)], inc=s_av)
                    else:
                        P.op("vector", lambda e, T=T: e.tensor_mul(
                            otmp[:, 0:T], p_av[0:64, 0:T], rb[0:64, 0:T]),
                            waits=[(s_nrm, s_nrm.n), (s_sh, s_sh.n)], inc=s_av)
                        P.op("sync", lambda e, loc=loc, T=T: e.dma_start(
                            out=ATTt[64:128, loc:loc + T], in_=otmp[:, 0:T]),
                            waits=[(s_av, s_av.n)], inc=s_sh, inc_n=16)
        P.wait("tensor", s_sh, s_sh.n)
        P.wait("tensor", s_av, s_av.n)
        for dc in range(8):
            for c in range(8):
                P.op("tensor", lambda e, dc=dc, c=c: e.matmul(
                    p_qkv[:, :],
                    wb[1][:, c * 1024 + dc * 128: c * 1024 + (dc + 1) * 128],
                    ATTt[:, c * W:(c + 1) * W],
                    start=(c == 0), stop=(c == 7)),
                    waits=[(s_wg, wol)] if c == 0 else (), inc=(s_pq if c == 7 else None))
            st_prev = s_st.n
            P.op("scalar", lambda e, dc=dc: e.activation(
                ln.tmp[:, :], p_qkv[:, :], AF.Identity, bias=bo_s[:, dc:dc + 1]),
                waits=[(s_pq, s_pq.n), (s_st, st_prev)], inc=s_ev)
            P.wait("tensor", s_ev, s_ev.n)
            P.op("vector", lambda e, dc=dc: e.tensor_add(ln.tmp[:, :], ln.tmp[:, :], xt[dc][:, :]),
                 waits=[(s_ev, s_ev.n)], inc=s_y)
            P.op("sync", lambda e, dc=dc, j=j: e.dma_start(
                out=x1T[dc * 128:(dc + 1) * 128, j * W:(j + 1) * W], in_=ln.tmp[:, :]),
                waits=[(s_y, s_y.n)], inc=s_st, inc_n=16)
        P.barrier()
    P.wait("sync", s_st, s_st.n)
    P.barrier()



def emit_l0_ffn(nc, ctx, P, C, hbm, xin="x1T", xout="x2T"):
    W = 512
    NT = T_TOK // W
    xT = hbm[xin].ap()
    xoT = hbm[xout].ap()
    wexp = hbm["W_exp"].ap()
    wconT = hbm["W_conT"].ap()
    uw = hbm["uw"].ap()
    dw = hbm["dw"].ap()
    bexp = hbm["b_exp"].ap()
    ubias = hbm["up_bias"].ap()
    bconv = hbm["b_con2"].ap()
    dbias = hbm["down_bias2"].ap()

    w1h = nc.dram_tensor("f_w1h", [128, 8 * 4096], BF16, kind="Internal").ap()
    w2h = nc.dram_tensor("f_w2h", [128, 32 * 1024], BF16, kind="Internal").ap()
    b1c = sbuf(ctx, nc, "f_b1c", [128, 32], F32)
    b2c = sbuf(ctx, nc, "f_b2c", [128, 8], F32)
    o1x128 = sbuf(ctx, nc, "f_o1x", [1, 128], F32)
    o128x1 = sbuf(ctx, nc, "f_o128", [128, 1], F32)
    idf32 = sbuf(ctx, nc, "f_idf", [128, 128], F32)
    onecell = sbuf(ctx, nc, "f_one1", [1, 1], F32)

    p_bc4 = psum(ctx, nc, "f_pbc4", [128, 2048])
    p_mm = [psum(ctx, nc, f"f_pmm{i}", [128, W]) for i in range(2)]
    p_t = psum(ctx, nc, "f_pt", [128, 128])

    s_cl = Sem(ctx, nc, "f_scl")
    s_row = Sem(ctx, nc, "f_srow")
    s_bc = Sem(ctx, nc, "f_sbc")
    s_v = Sem(ctx, nc, "f_sv")
    s_tr = Sem(ctx, nc, "f_str")
    s_x = Sem(ctx, nc, "f_sx")
    s_mm = Sem(ctx, nc, "f_smm")
    s_ev = Sem(ctx, nc, "f_sev")
    s_y = Sem(ctx, nc, "f_sy")
    s_st = Sem(ctx, nc, "f_sst")

    for _ld in (lambda e: e.dma_start(out=o1x128[:, :], in_=C["ones1x128"].ap()[:, :]),
                lambda e: e.dma_start(out=o128x1[:, :], in_=C["ones128x1"].ap()[:, :]),
                lambda e: e.dma_start(out=idf32[:, :], in_=C["identity"].ap()[:, :]),
                lambda e: e.dma_start(out=onecell[:, :], in_=C["ones1x128"].ap()[:, 0:1]),
                lambda e: e.dma_start(out=b2c[:, :], in_=bconv),
                lambda e: e.dma_start(out=b1c[:, 0:8], in_=dbias)):
        P.op("sync", _ld, waits=[(s_row, s_row.n)], inc=s_row, inc_n=16)
    cstv = s_row.n
    P.op("vector", lambda e: e.tensor_add(b2c[:, :], b2c[:, :], b1c[:, 0:8]),
         waits=[(s_row, s_row.n)], inc=s_v)
    P.barrier()

    with ExitStack() as c2:
        X = sbuf(c2, nc, "f_X", [128, 4096], F32)
        nascr = sbuf(c2, nc, "f_na", [128, 2048], F32)
        tmp2 = sbuf(c2, nc, "f_t2", [128, 2048], F32)
        wcbP = [[sbuf(c2, nc, f"f_wcb{bb}_{k}", [128, 2048], F32) for k in range(4)]
                for bb in range(2)]
        w2st = sbuf(c2, nc, "f_w2st", [128, 32 * 128], BF16)

        free_v = [0, 0]
        last_acv = [0]
        eW = {}

        def stage_weights(wsrc, stage, comps, bb):
            for k in range(4):
                ridx = stage * 4 + comps[k]
                P.op("sync", lambda e, ridx=ridx, wsrc=wsrc, k=k, bb=bb: e.dma_start(
                    out=wcbP[bb][k][:, :],
                    in_=wsrc.rearrange("s c x -> (s c) x")[ridx:ridx + 1, :]
                        .partition_broadcast(128)),
                    waits=[(s_row, s_row.n), (s_v, free_v[bb])],
                    inc=s_row, inc_n=16)
            return s_row.n

        def stage_apply(Xap, stage, p, bb, ew):
            s = 2 ** stage
            Xv = Xap.rearrange("p (nb two s) -> p nb two s", two=2, s=s)
            a = Xv[:, :, 0, :]
            b = Xv[:, :, 1, :]
            wv = [wcbP[bb][k][0:p, :].rearrange("p (nb s) -> p nb s", s=s)
                  for k in range(4)]
            na = nascr[0:p, :].rearrange("p (nb s) -> p nb s", s=s)
            t2 = tmp2[0:p, :].rearrange("p (nb s) -> p nb s", s=s)
            P.op("vector", lambda e: e.tensor_mul(t2, b, wv[1]),
                 waits=[(s_row, ew), (s_ev, last_acv[0])])
            P.op("vector", lambda e: e.tensor_mul(na, a, wv[0]))
            P.op("vector", lambda e: e.tensor_add(na, na, t2))
            P.op("vector", lambda e: e.tensor_mul(t2, b, wv[3]))
            P.op("vector", lambda e: e.tensor_mul(b, a, wv[2]))
            P.op("vector", lambda e: e.tensor_add(b, b, t2), inc=s_v)
            free_v[bb] = s_v.n
            P.op("scalar", lambda e: e.activation(a, na, AF.Identity),
                 waits=[(s_v, s_v.n)], inc=s_ev)
            last_acv[0] = s_ev.n

        def compose_pass(wsrc, st_list, comps, p):
            ew_next = stage_weights(wsrc, st_list[0], comps, 0)
            for si, stage in enumerate(st_list):
                ew_cur = ew_next
                if si + 1 < len(st_list):
                    ew_next = stage_weights(wsrc, st_list[si + 1], comps, (si + 1) % 2)
                stage_apply(X[0:p, :], stage, p, si % 2, ew_cur)

        for ci in range(9):
            p = 128 if ci < 8 else 1
            if ci < 8:
                lv = P.op("gpsimd", lambda e, ci=ci: e.dma_start(
                    out=X[:, :], in_=wexp[ci * 128:(ci + 1) * 128, :]),
                    waits=[(s_v, s_v.n), (s_cl, s_cl.n)], inc=s_cl, inc_n=16)
            else:
                lv = P.op("gpsimd", lambda e: e.dma_start(out=X[0:1, :], in_=bexp),
                          waits=[(s_v, s_v.n), (s_cl, s_cl.n)], inc=s_cl, inc_n=16)
            P.wait("vector", s_cl, lv)
            P.wait("scalar", s_cl, lv)
            compose_pass(uw, list(range(12)), (0, 1, 2, 3), p)
            if ci < 8:
                P.op("gpsimd", lambda e, ci=ci: e.dma_start(
                    out=w1h[:, ci * 4096:(ci + 1) * 4096], in_=X[:, :]),
                    waits=[(s_v, s_v.n), (s_ev, last_acv[0]), (s_cl, s_cl.n)],
                    inc=s_cl, inc_n=16)
            else:
                lv2 = P.op("sync", lambda e: e.dma_start(out=nascr[0:1, :], in_=ubias[:, 0:2048]),
                           waits=[(s_row, s_row.n), (s_v, s_v.n), (s_ev, last_acv[0])],
                           inc=s_row, inc_n=16)
                lv2 = P.op("sync", lambda e: e.dma_start(out=tmp2[0:1, :], in_=ubias[:, 2048:4096]),
                           waits=[(s_row, s_row.n), (s_v, s_v.n), (s_ev, last_acv[0])],
                           inc=s_row, inc_n=16)
                P.op("vector", lambda e: e.tensor_add(X[0:1, 0:2048], X[0:1, 0:2048], nascr[0:1, :]),
                     waits=[(s_row, lv2), (s_ev, last_acv[0])])
                P.op("vector", lambda e: e.tensor_add(X[0:1, 2048:4096], X[0:1, 2048:4096], tmp2[0:1, :]),
                     inc=s_v)
                for i in range(32):
                    P.op("tensor", lambda e, i=i: e.matmul(
                        p_t[:, 0:1], X[0:1, i * 128:(i + 1) * 128], onecell[:, :],
                        start=True, stop=True),
                        waits=[(s_v, s_v.n), (s_tr, s_tr.n)], inc=s_bc)
                    P.op("vector", lambda e, i=i: e.tensor_copy(b1c[:, i:i + 1], p_t[:, 0:1]),
                         waits=[(s_bc, s_bc.n)], inc=s_tr)
                P.wait("tensor", s_tr, s_tr.n)
        for ci in range(8):
            lv = P.op("gpsimd", lambda e, ci=ci: e.dma_start(
                out=X[:, :], in_=wconT[ci * 128:(ci + 1) * 128, :]),
                waits=[(s_v, s_v.n), (s_tr, s_tr.n), (s_ev, last_acv[0]),
                       (s_cl, s_cl.n)], inc=s_cl, inc_n=16)
            P.wait("vector", s_cl, lv)
            P.wait("scalar", s_cl, lv)
            compose_pass(dw, list(range(11, -1, -1)), (0, 2, 1, 3), 128)
            xv = s_v.n
            xa = last_acv[0]
            for hc in range(32):
                P.transpose_l(lambda e, hc=hc: (p_t[:, :], X[:, hc * 128:(hc + 1) * 128], idf32[:, :]),
                              s_bc, waits=[(s_v, xv), (s_ev, xa), (s_tr, s_tr.n),
                                           (s_row, s_row.n)])
                P.op("vector", lambda e, hc=hc: e.tensor_copy(
                    w2st[:, hc * 128:(hc + 1) * 128], p_t[:, :]),
                    waits=[(s_bc, s_bc.n)], inc=s_tr)
            P.wait("tensor", s_tr, s_tr.n)
            for a0 in range(0, 32, 8):
                P.op("sync", lambda e, ci=ci, a0=a0: e.dma_start(
                    out=w2h.rearrange("p (a d) -> p a d", a=32)[:, a0:a0 + 8,
                                                               ci * 128:(ci + 1) * 128],
                    in_=w2st[:, a0 * 128:(a0 + 8) * 128].rearrange(
                        "p (a d) -> p a d", a=8)),
                    waits=[(s_tr, s_tr.n), (s_row, s_row.n)], inc=s_row, inc_n=16)
        P.wait("gpsimd", s_row, s_row.n)
        P.barrier()

    with ExitStack() as c3:
        w1s = sbuf(c3, nc, "f_w1", [128, 8 * 4096], BF16)
        w2s = sbuf(c3, nc, "f_w2", [128, 32 * 1024], BF16)
        xt = [sbuf(c3, nc, f"f_x{c}", [128, W], F32) for c in range(8)]
        lnx = [sbuf(c3, nc, f"f_ln{c}", [128, W], BF16) for c in range(8)]
        gT = sbuf(c3, nc, "f_gT", [128, 32 * W], BF16)
        ln = LNKit(c3, nc, P, C, W=W, tag="f_lnk", ones_col=o128x1, ones_row=o1x128,
                   psums=(p_bc4[0:33, 0:512], p_bc4[:, 512:1024], p_bc4[:, 1024:1536]))
        for buf, src_h in ((w1s, w1h), (w2s, w2h)):
            for hh in range(2):
                P.op("sync", lambda e, buf=buf, src_h=src_h, hh=hh: e.dma_start(
                    out=buf[:, hh * 16384:(hh + 1) * 16384],
                    in_=src_h[:, hh * 16384:(hh + 1) * 16384]),
                    waits=[(s_row, s_row.n)], inc=s_row, inc_n=16)
        wlv = s_row.n
        P.wait("tensor", s_row, wlv)
        for j in range(NT):
            for c in range(8):
                P.op("sync", lambda e, c=c, j=j: e.dma_start(
                    out=xt[c][:, :], in_=xT[c * 128:(c + 1) * 128, j * W:(j + 1) * W]),
                    waits=[(s_st, s_st.n)], inc=s_x, inc_n=16)
            xw = [(s_x, s_x.n)] * 8
            P.wait("vector", s_st, s_st.n)
            ln.emit([xt[c][:, :] for c in range(8)],
                    [lnx[c][:, :] for c in range(8)], xw)
            P.barrier()
            for hc in range(32):
                pp = hc % 2
                for c in range(8):
                    P.op("tensor", lambda e, hc=hc, c=c, pp=pp: e.matmul(
                        p_mm[pp][:, :],
                        w1s[:, c * 4096 + hc * 128: c * 4096 + (hc + 1) * 128],
                        lnx[c][:, :], start=(c == 0), stop=(c == 7)),
                        waits=[(s_ev, s_ev.n - 1)], inc=(s_mm if c == 7 else None))
                P.op("scalar", lambda e, hc=hc, pp=pp: e.activation(
                    gT[:, hc * W:(hc + 1) * W], p_mm[pp][:, :], AF.Gelu,
                    bias=b1c[:, hc:hc + 1]),
                    waits=[(s_mm, s_mm.n)], inc=s_ev)
            for dc in range(8):
                pp = dc % 2
                for hc in range(32):
                    P.op("tensor", lambda e, dc=dc, hc=hc, pp=pp: e.matmul(
                        p_mm[pp][:, :],
                        w2s[:, hc * 1024 + dc * 128: hc * 1024 + (dc + 1) * 128],
                        gT[:, hc * W:(hc + 1) * W], start=(hc == 0), stop=(hc == 31)),
                        waits=[(s_ev, s_ev.n), (s_y, s_y.n - 1)],
                        inc=(s_mm if hc == 31 else None))
                st_prev = s_st.n
                P.op("scalar", lambda e, dc=dc, pp=pp: e.activation(
                    ln.tmp[:, :], p_mm[pp][:, :], AF.Identity, bias=b2c[:, dc:dc + 1]),
                    waits=[(s_mm, s_mm.n), (s_st, st_prev)], inc=s_ev)
                P.op("vector", lambda e, dc=dc: e.tensor_add(ln.tmp[:, :], ln.tmp[:, :], xt[dc][:, :]),
                     waits=[(s_ev, s_ev.n)], inc=s_y)
                P.op("sync", lambda e, dc=dc, j=j: e.dma_start(
                    out=xoT[dc * 128:(dc + 1) * 128, j * W:(j + 1) * W], in_=ln.tmp[:, :]),
                    waits=[(s_y, s_y.n)], inc=s_st, inc_n=16)
            P.barrier()
        P.wait("sync", s_st, s_st.n)
        P.barrier()


def l0_ffn_inputs(ins):
    uw = np.ascontiguousarray(
        ins["L0_ffn_up_weights"].reshape(12, 2048, 4).transpose(0, 2, 1))
    dw = np.ascontiguousarray(
        ins["L0_ffn_down_weights"].reshape(12, 2048, 4).transpose(0, 2, 1))
    m = {
        "W_exp": ins["L0_ffn_W_exp"],
        "W_conT": np.ascontiguousarray(ins["L0_ffn_W_con"].T),
        "uw": uw, "dw": dw,
        "b_exp": ins["L0_ffn_b_exp"].reshape(1, 4096),
        "up_bias": ins["L0_ffn_up_bias"].reshape(1, 4096),
        "b_con2": np.ascontiguousarray(ins["L0_ffn_b_con"].reshape(8, 128).T),
        "down_bias2": np.ascontiguousarray(ins["L0_ffn_down_bias"].reshape(8, 128).T),
    }
    return {k: np.ascontiguousarray(v.astype(np.float32)) for k, v in m.items()}


def build_l0_ffn_nc():
    nc = bass.Bass("TRN2", target_bir_lowering=False, dynamic_dma_scratch_size=16384)
    hbm = {}
    hbm["x1T"] = nc.dram_tensor("x1T", [D, T_TOK], F32, kind="ExternalInput")
    hbm["x2T"] = nc.dram_tensor("x2T", [D, T_TOK], F32, kind="ExternalOutput")
    for nm, shp in (("W_exp", [D, 4096]), ("W_conT", [D, 4096]),
                    ("uw", [12, 4, 2048]), ("dw", [12, 4, 2048]),
                    ("b_exp", [1, 4096]), ("up_bias", [1, 4096]),
                    ("b_con2", [128, 8]), ("down_bias2", [128, 8])):
        hbm[nm] = nc.dram_tensor(nm, shp, F32, kind="ExternalInput")
    with ExitStack() as ctx:
        C = make_consts(nc)
        P = Prog(nc, ctx)
        emit_l0_ffn(nc, ctx, P, C, hbm)
        P.run()
    return nc



def emit_l1_attn(nc, ctx, P, C, hbm, xin="x2T", xout="x3T"):
    W = 512
    NT = T_TOK // W
    NE = 35
    xT = hbm[xin].ap()
    xoT = hbm[xout].ap()
    szT = hbm["szT"].ap()
    sxT = hbm["sxT"].ap()
    y1T = hbm["y1T"].ap()

    o1x128 = sbuf(ctx, nc, "q_o1x", [1, 128], F32)
    o128x1 = sbuf(ctx, nc, "q_o128", [128, 1], F32)
    idf32 = sbuf(ctx, nc, "q_idf", [128, 128], F32)
    idbf = sbuf(ctx, nc, "q_idb", [128, 128], BF16)
    negA = sbuf(ctx, nc, "q_negA", [32, 1], F32)
    dtb_s = sbuf(ctx, nc, "q_dtb", [32, 1], F32)
    Drep = sbuf(ctx, nc, "q_Drep", [128, 16], F32)
    c12 = ExitStack()
    dtall = sbuf(c12, nc, "q_dt", [32, T_TOK], F32)
    lndtall = sbuf(c12, nc, "q_lndt", [32, T_TOK], F32)
    dtAall = sbuf(c12, nc, "q_dtA", [32, T_TOK], F32)

    s_w = Sem(ctx, nc, "q_sw")
    s_x = Sem(ctx, nc, "q_sx")
    s_mm = Sem(ctx, nc, "q_smm")
    s_ev = Sem(ctx, nc, "q_sev")
    s_st = Sem(ctx, nc, "q_sst")
    s_v = Sem(ctx, nc, "q_sv")
    s_ld = Sem(ctx, nc, "q_sld")

    def sload(fn):
        P.op("sync", fn, waits=[(s_w, s_w.n)], inc=s_w, inc_n=16)

    def gload(fn):
        P.op("gpsimd", fn, waits=[(s_ld, s_ld.n)], inc=s_ld, inc_n=16)

    sload(lambda e: e.dma_start(out=o1x128[:, :], in_=C["ones1x128"].ap()[:, :]))
    sload(lambda e: e.dma_start(out=o128x1[:, :], in_=C["ones128x1"].ap()[:, :]))
    sload(lambda e: e.dma_start(out=idf32[:, :], in_=C["identity"].ap()[:, :]))
    sload(lambda e: e.dma_start(out=idbf[:, :], in_=C["ident_bf"].ap()[:, :]))
    sload(lambda e: e.dma_start(out=dtb_s[:, :], in_=hbm["dtb"].ap()))
    sload(lambda e: e.dma_start(out=negA[:, :], in_=hbm["A_log"].ap()))
    wv0 = s_w.n
    P.op("scalar", lambda e: e.activation(negA[:, :], negA[:, :], AF.Exp),
         waits=[(s_w, wv0)], inc=s_ev)
    P.op("vector", lambda e: e.tensor_scalar(negA[:, :], negA[:, :], -1.0, None, ALU.mult),
         waits=[(s_ev, s_ev.n)], inc=s_v)
    P.barrier()

    with ExitStack() as c1:
        ipT = sbuf(c1, nc, "q_ipT", [128, 8 * 4384], BF16)
        nw_s = sbuf(c1, nc, "q_nw", [128, 8], F32)
        convw = sbuf(c1, nc, "q_cw", [128, 72], F32)
        convb = sbuf(c1, nc, "q_cb", [128, 18], F32)
        convin = sbuf(c1, nc, "q_cin", [128, 18 * 516], BF16)
        ctmp = sbuf(c1, nc, "q_ctmp", [128, W], F32)
        sout = sbuf(c1, nc, "q_sout", [128, W], BF16)
        dstage = sbuf(c1, nc, "q_dst", [32, W], F32)
        dst2 = sbuf(c1, nc, "q_dst2", [32, W], F32)
        xt = [sbuf(c1, nc, f"q_x{c}", [128, W], F32) for c in range(8)]
        lnx = [sbuf(c1, nc, f"q_ln{c}", [128, W], BF16) for c in range(8)]
        p_mm = [psum(c1, nc, f"q_pmm{i}", [128, W]) for i in range(2)]
        p_aux = psum(c1, nc, "q_paux", [128, W])
        ln = LNKit(c1, nc, P, C, W=W, tag="q_lnk", ones_col=o128x1, ones_row=o1x128,
                   psums=(p_aux[0:33, :], p_mm[0][:, :], p_mm[1][:, :]))
        sload(lambda e: e.dma_start(out=nw_s[:, :], in_=hbm["norm_w"].ap()))
        sload(lambda e: e.dma_start(out=convw[:, :], in_=hbm["convw"].ap()))
        sload(lambda e: e.dma_start(out=convb[:, :], in_=hbm["convb"].ap()))
        gload(lambda e: e.dma_start(
            out=ipT[:, :].rearrange("p (a d) -> p a d", a=8),
            in_=hbm["in_projT"].ap().rearrange("(a p) d -> p a d", p=128)))
        wv = s_w.n
        lv = s_ld.n
        P.wait("vector", s_w, wv)
        P.wait("vector", s_ld, lv)
        for c in range(8):
            P.op("vector", lambda e, c=c: e.tensor_scalar(
                ipT[:, c * 4384:(c + 1) * 4384], ipT[:, c * 4384:(c + 1) * 4384],
                nw_s[:, c:c + 1], None, ALU.mult), inc=s_v)
        P.op("vector", lambda e: e.memset(convin[:, :], 0.0), inc=s_v)
        P.barrier()

        for j in range(NT):
            for c in range(8):
                P.op("sync", lambda e, c=c, j=j: e.dma_start(
                    out=xt[c][:, :], in_=xT[c * 128:(c + 1) * 128, j * W:(j + 1) * W]),
                    waits=[(s_x, s_x.n)], inc=s_x, inc_n=16)
            xw = [(s_x, s_x.n)] * 8
            P.wait("vector", s_st, s_st.n)
            ln.emit([xt[c][:, :] for c in range(8)],
                    [lnx[c][:, :] for c in range(8)], xw)
            P.barrier()
            for ec in range(NE):
                rows = 128 if ec < 34 else 32
                pp = ec % 2
                for c in range(8):
                    P.op("tensor", lambda e, ec=ec, c=c, pp=pp, rows=rows: e.matmul(
                        p_mm[pp][0:rows, :],
                        ipT[:, c * 4384 + ec * 128: c * 4384 + ec * 128 + rows],
                        lnx[c][:, :], start=(c == 0), stop=(c == 7)),
                        waits=[(s_ev, s_ev.n)], inc=(s_mm if c == 7 else None))
                mmv = s_mm.n
                if ec < 16:
                    P.op("scalar", lambda e, pp=pp: e.activation(
                        sout[:, :], p_mm[pp][:, :], AF.Silu),
                        waits=[(s_mm, mmv), (s_st, s_st.n)], inc=s_ev)
                    P.op("sync", lambda e, ec=ec, j=j: e.dma_start(
                        out=szT[ec * 128:(ec + 1) * 128, j * W:(j + 1) * W], in_=sout[:, :]),
                        waits=[(s_ev, s_ev.n), (s_st, s_st.n)], inc=s_st, inc_n=16)
                elif ec < 34:
                    cc = ec - 16
                    P.op("vector", lambda e, pp=pp, cc=cc: e.tensor_copy(
                        convin[:, cc * 516 + 3: cc * 516 + 515], p_mm[pp][:, :]),
                        waits=[(s_mm, mmv)], inc=s_ev)
                else:
                    P.op("vector", lambda e, pp=pp: e.tensor_copy(
                        dstage[:, :], p_mm[pp][0:32, :]),
                        waits=[(s_mm, mmv)], inc=s_ev)
            P.barrier()
            for cc in range(18):
                base = cc * 516
                P.op("scalar", lambda e, cc=cc, base=base: e.activation(
                    ctmp[:, :], convin[:, base:base + W], AF.Identity,
                    scale=convw[:, cc * 4:cc * 4 + 1]),
                    waits=[(s_st, s_st.n)], inc=s_ev)
                for k in (1, 2, 3):
                    P.op("vector", lambda e, cc=cc, base=base, k=k: e.scalar_tensor_tensor(
                        ctmp[:, :], convin[:, base + k:base + k + W],
                        convw[:, cc * 4 + k:cc * 4 + k + 1], ctmp[:, :],
                        ALU.mult, ALU.add),
                        waits=[(s_ev, s_ev.n)], inc=(s_v if k == 3 else None))
                P.op("scalar", lambda e, cc=cc: e.activation(
                    sout[:, :], ctmp[:, :], AF.Silu, bias=convb[:, cc:cc + 1]),
                    waits=[(s_v, s_v.n)], inc=s_ev)
                P.op("sync", lambda e, cc=cc, j=j: e.dma_start(
                    out=sxT[cc * 128:(cc + 1) * 128, j * W:(j + 1) * W], in_=sout[:, :]),
                    waits=[(s_ev, s_ev.n), (s_st, s_st.n)], inc=s_st, inc_n=16)
                P.op("vector", lambda e, base=base: e.tensor_copy(
                    convin[:, base:base + 3], convin[:, base + 512:base + 515]),
                    waits=[(s_ev, s_ev.n)], inc=s_v)
            P.op("vector", lambda e: e.tensor_scalar(dstage[:, :], dstage[:, :],
                                                     dtb_s[:, 0:1], None, ALU.add))
            P.op("vector", lambda e: e.tensor_copy(dst2[:, :], dstage[:, :]), inc=s_v)
            P.op("scalar", lambda e: e.activation(dst2[:, :], dst2[:, :], AF.Abs),
                 waits=[(s_v, s_v.n)])
            P.op("scalar", lambda e: e.activation(dst2[:, :], dst2[:, :], AF.Exp,
                                                  scale=np.float32(-1.0)))
            P.op("scalar", lambda e: e.activation(dst2[:, :], dst2[:, :], AF.Ln,
                                                  bias=np.float32(1.0)), inc=s_ev)
            P.op("vector", lambda e: e.tensor_scalar(dstage[:, :], dstage[:, :], 0.0,
                                                     None, ALU.max),
                 waits=[(s_ev, s_ev.n)])
            P.op("vector", lambda e, j=j: e.tensor_add(
                dtall[:, j * W:(j + 1) * W], dstage[:, :], dst2[:, :]), inc=s_v)
            P.op("scalar", lambda e, j=j: e.activation(
                lndtall[:, j * W:(j + 1) * W], dtall[:, j * W:(j + 1) * W], AF.Ln),
                waits=[(s_v, s_v.n)], inc=s_ev)
            P.op("vector", lambda e, j=j: e.tensor_scalar(
                dtAall[:, j * W:(j + 1) * W], dtall[:, j * W:(j + 1) * W],
                negA[:, 0:1], None, ALU.mult), inc=s_v)
            P.barrier()
        P.wait("sync", s_st, s_st.n)
        P.barrier()

    with ExitStack() as c2:
        opT = sbuf(c2, nc, "q_opT", [128, 16 * 1024], BF16)
        ssmw = sbuf(c2, nc, "q_ssmw", [128, 16], F32)
        Dh_s = sbuf(c2, nc, "q_Dh", [32, 1], F32)
        mk16 = sbuf(c2, nc, "q_mk16", [32, 16], F32)
        parc = sbuf(c2, nc, "q_parc", [32, 128], F32)
        tri_f = sbuf(c2, nc, "q_trif", [128, 128], F32)
        tle_b = sbuf(c2, nc, "q_tle", [128, 128], BF16)
        i32f = sbuf(c2, nc, "q_i32", [32, 32], F32)
        o32bf = sbuf(c2, nc, "q_o32b", [32, 128], BF16)
        o32f = sbuf(c2, nc, "q_o32f", [32, 128], F32)
        H_all = sbuf(c2, nc, "q_H", [128, 32 * 64], F32)
        Hb = sbuf(c2, nc, "q_Hb", [128, 32 * 64], BF16)
        sxc = sbuf(c2, nc, "q_sxc", [128, 18 * 128], BF16)
        szc = sbuf(c2, nc, "q_szc", [128, 16 * 128], BF16)
        xtok = sbuf(c2, nc, "q_xtok", [128, 16 * 128], BF16)
        Btok = sbuf(c2, nc, "q_Btok", [128, 128], BF16)
        CBm = sbuf(c2, nc, "q_CBm", [128, 128], BF16)
        Dp = sbuf(c2, nc, "q_Dp", [128, 128], BF16)
        Ct = sbuf(c2, nc, "q_Ct", [128, 128], BF16)
        expX = sbuf(c2, nc, "q_expX", [128, 128], BF16)
        Gm = sbuf(c2, nc, "q_Gm", [128, 128], BF16)
        xsc = sbuf(c2, nc, "q_xsc", [128, 64], BF16)
        argb = sbuf(c2, nc, "q_argb", [128, 128], F32)
        clT = sbuf(c2, nc, "q_clT", [128, 32], F32)
        biasT = sbuf(c2, nc, "q_biasT", [128, 32], F32)
        wT = sbuf(c2, nc, "q_wT", [128, 32], F32)
        cl_r = sbuf(c2, nc, "q_clr", [32, 128], F32)
        wrm = sbuf(c2, nc, "q_wrm", [32, 128], F32)
        dgh = sbuf(c2, nc, "q_dgh", [32, 128], F32)
        dg32 = sbuf(c2, nc, "q_dg32", [32, 32], F32)
        gamb = sbuf(c2, nc, "q_gamb", [128, 32], F32)
        ecl = sbuf(c2, nc, "q_ecl", [32, 1], F32)
        yb = sbuf(c2, nc, "q_yb", [128, 16 * 128], BF16)
        gbuf = sbuf(c2, nc, "q_gbuf", [128, 16 * 128], BF16)
        sqb = sbuf(c2, nc, "q_sqb", [128, 16 * 128], F32)
        rstd1 = sbuf(c2, nc, "q_rstd1", [1, 128], F32)
        gnall = sbuf(c2, nc, "q_gnall", [128, 16 * W], BF16)
        xt2 = [sbuf(c2, nc, f"q_x2{c}", [128, W], F32) for c in range(8)]
        lnx2 = [sbuf(c2, nc, f"q_ln2{c}", [128, W], BF16) for c in range(8)]
        ytile = sbuf(c2, nc, "q_yt", [128, W], F32)

        p_x = psum(c2, nc, "q_px", [128, 128])
        p_cb = psum(c2, nc, "q_pcb", [128, 128])
        p_y = psum(c2, nc, "q_py", [128, 128])
        p_dh = psum(c2, nc, "q_pdh", [128, 64])
        p_tr = psum(c2, nc, "q_ptr", [128, 128])
        p_trb = psum(c2, nc, "q_ptrb", [128, 128], BF16)
        p_o = [psum(c2, nc, f"q_po{i}", [128, W]) for i in range(2)]
        s_cl = Sem(c2, nc, "q_scl")
        s_cx = Sem(c2, nc, "q_scx")

        def tfence(sem, waits=()):
            return P.op("tensor", lambda e: e.matmul(
                p_tr[64:65, 127:128], o128x1[:, :], o128x1[:, :],
                start=True, stop=True), waits=waits, inc=sem)
        ln2 = LNKit(c2, nc, P, C, W=W, tag="q_ln2k", ones_col=o128x1, ones_row=o1x128,
                    psums=(p_o[0][0:33, :], p_o[0][:, :], p_o[1][:, :]))

        sload(lambda e: e.dma_start(out=ssmw[:, :], in_=hbm["ssmw"].ap()))
        sload(lambda e: e.dma_start(out=Dh_s[:, :], in_=hbm["Dvec"].ap()))
        sload(lambda e: e.dma_start(out=mk16[:, :], in_=C["mask16"].ap()[:, :]))
        sload(lambda e: e.dma_start(out=parc[:, :], in_=C["parc"].ap()[:, :]))
        sload(lambda e: e.dma_start(out=tri_f[:, :], in_=C["tri_incl"].ap()[:, :]))
        sload(lambda e: e.dma_start(out=tle_b[:, :], in_=C["tri_le_bf"].ap()[:, :]))
        sload(lambda e: e.dma_start(out=i32f[:, :], in_=C["i32"].ap()[:, :]))
        sload(lambda e: e.dma_start(out=o32bf[:, :], in_=C["ones32x128b"].ap()[:, :]))
        sload(lambda e: e.dma_start(out=o32f[:, :], in_=C["ones32x128f"].ap()[:, :]))
        gload(lambda e: e.dma_start(
            out=opT[:, :].rearrange("p (a d) -> p a d", a=16),
            in_=hbm["out_projT"].ap().rearrange("(a p) d -> p a d", p=128)))
        wv = s_w.n
        lv2 = s_ld.n
        P.wait("vector", s_w, wv)
        P.wait("vector", s_ld, lv2)
        P.wait("tensor", s_w, wv)
        P.wait("scalar", s_w, wv)
        for ec in range(16):
            P.op("vector", lambda e, ec=ec: e.tensor_scalar(
                opT[:, ec * 1024:(ec + 1) * 1024], opT[:, ec * 1024:(ec + 1) * 1024],
                ssmw[:, ec:ec + 1], None, ALU.mult), inc=s_v)
        P.op("vector", lambda e: e.tensor_scalar(mk16[:, :], mk16[:, :], Dh_s[:, 0:1],
                                                 None, ALU.mult), inc=s_v)
        P.op("tensor", lambda e: e.matmul(p_x[:, 0:16], parc[:, :], mk16[:, :],
                                          start=True, stop=True),
             waits=[(s_v, s_v.n)], inc=s_mm)
        P.op("vector", lambda e: e.tensor_copy(Drep[:, :], p_x[:, 0:16]),
             waits=[(s_mm, s_mm.n)], inc=s_v)
        P.op("vector", lambda e: e.memset(H_all[:, :], 0.0), inc=s_v)
        if hbm.get("dbgY") is not None:
            P.op("sync", lambda e: e.dma_start(out=hbm["dbgD"].ap()[0:32, :], in_=dtall[:, :]),
                 waits=[(s_st, s_st.n)], inc=s_st, inc_n=16)
            P.op("sync", lambda e: e.dma_start(out=hbm["dbgD"].ap()[32:64, :], in_=lndtall[:, :]),
                 waits=[(s_st, s_st.n)], inc=s_st, inc_n=16)
            P.op("sync", lambda e: e.dma_start(out=hbm["dbgD"].ap()[64:96, :], in_=dtAall[:, :]),
                 waits=[(s_st, s_st.n)], inc=s_st, inc_n=16)
            P.wait("sync", s_st, s_st.n)
        P.barrier()

        import os as _os9
        M2_LN = _os9.environ.get("M2_LN", "1") == "1"
        M2_LOADS = _os9.environ.get("M2_LOADS", "1") == "1"
        for j in range(NT):
            if M2_LN:
                for c in range(8):
                    P.op("sync", lambda e, c=c, j=j: e.dma_start(
                        out=xt2[c][:, :], in_=xT[c * 128:(c + 1) * 128, j * W:(j + 1) * W]),
                        waits=[(s_x, s_x.n)], inc=s_x, inc_n=16)
                xw = [(s_x, s_x.n)] * 8
                P.wait("vector", s_st, s_st.n)
                ln2.emit([xt2[c][:, :] for c in range(8)],
                         [lnx2[c][:, :] for c in range(8)], xw)
            P.barrier()
            for cc4 in range(4):
                ch = j * 4 + cc4
                t0 = ch * 128
                if M2_LOADS:
                    P.op("sync", lambda e, t0=t0: e.dma_start(
                        out=sxc[:, :].rearrange("p (a d) -> p a d", a=18),
                        in_=sxT.rearrange("(a p) d -> p a d", p=128)[:, :, t0:t0 + 128]),
                        waits=[(s_cx, s_cx.n), (s_v, s_v.n)], inc=s_cx, inc_n=16)
                    P.op("sync", lambda e, t0=t0: e.dma_start(
                        out=szc[:, :].rearrange("p (a d) -> p a d", a=16),
                        in_=szT.rearrange("(a p) d -> p a d", p=128)[:, :, t0:t0 + 128]),
                        waits=[(s_cx, s_cx.n)], inc=s_cx, inc_n=16)
                ldv = s_cx.n
                P.op("vector", lambda e: e.tensor_copy(Hb[:, :], H_all[:, :]), inc=s_v)
                P.transpose_l(lambda e, t0=t0: (p_tr[:, 0:32], dtAall[:, t0:t0 + 128], idf32[0:32, 0:32]), s_cl, waits=[(s_v, s_v.n)])
                P.op("vector", lambda e: e.tensor_copy(clT[:, :], p_tr[:, 0:32]),
                     waits=[(s_cl, s_cl.n)], inc=s_v)
                P.op("tensor", lambda e: e.matmul(
                    p_tr[:, 0:32], tri_f[:, :], clT[:, :], start=True, stop=True),
                    waits=[(s_v, s_v.n)], inc=s_cl)
                P.op("vector", lambda e: e.tensor_copy(clT[:, :], p_tr[:, 0:32]),
                     waits=[(s_cl, s_cl.n)], inc=s_v)
                P.transpose_l(lambda e: (p_tr[0:32, :], clT[:, :], idf32[:, :]), s_cl, waits=[(s_v, s_v.n)])
                P.op("vector", lambda e: e.tensor_copy(cl_r[:, :], p_tr[0:32, :]),
                     waits=[(s_cl, s_cl.n)], inc=s_v)
                P.op("vector", lambda e, t0=t0: e.tensor_sub(
                    wrm[:, :], lndtall[:, t0:t0 + 128], cl_r[:, :]), inc=s_v)
                P.transpose_l(lambda e: (p_tr[:, 0:32], wrm[:, :], idf32[0:32, 0:32]), s_cl, waits=[(s_v, s_v.n)])
                P.op("vector", lambda e: e.tensor_copy(biasT[:, :], p_tr[:, 0:32]),
                     waits=[(s_cl, s_cl.n)], inc=s_v)
                P.op("vector", lambda e: e.tensor_scalar(
                    wrm[:, :], wrm[:, :], cl_r[:, 127:128], None, ALU.add), inc=s_v)
                P.op("scalar", lambda e: e.activation(wrm[:, :], wrm[:, :], AF.Exp),
                     waits=[(s_v, s_v.n)], inc=s_ev)
                P.transpose_l(lambda e: (p_tr[:, 0:32], wrm[:, :], idf32[0:32, 0:32]), s_cl, waits=[(s_ev, s_ev.n)])
                P.op("vector", lambda e: e.tensor_copy(wT[:, :], p_tr[:, 0:32]),
                     waits=[(s_cl, s_cl.n)], inc=s_v)
                import os as _os3
                M2_TAIL = _os3.environ.get("M2_TAIL", "1") == "1"
                P.op("scalar", lambda e: e.activation(ecl[:, :], cl_r[:, 127:128], AF.Exp),
                     inc=s_ev)
                if M2_TAIL:
                    P.op("vector", lambda e: e.tensor_scalar(dg32[:, :], i32f[:, :],
                                                             ecl[:, 0:1], None, ALU.mult),
                         waits=[(s_ev, s_ev.n)], inc=s_v)
                    P.op("tensor", lambda e: e.matmul(
                        p_tr[:, 0:32], o32f[:, :], dg32[:, :], start=True, stop=True),
                        waits=[(s_v, s_v.n)], inc=s_cl)
                    P.op("vector", lambda e: e.tensor_copy(gamb[:, :], p_tr[:, 0:32]),
                         waits=[(s_cl, s_cl.n)], inc=s_v)
                import os as _os2
                M2_PREP = _os2.environ.get("M2_PREP", "1") == "1"
                if not M2_PREP:
                    pass
                P.transpose_l(lambda e: (p_trb[:, :], sxc[:, 16 * 128:17 * 128], idbf[:, :]), s_mm, waits=[(s_cx, ldv)])
                P.op("vector", lambda e: e.tensor_copy(Btok[:, :], p_trb[:, :]),
                     waits=[(s_mm, s_mm.n)], inc=s_v)
                for xc in range(16 if M2_PREP else 0):
                    P.transpose_l(lambda e, xc=xc: (p_trb[:, :], sxc[:, xc * 128:(xc + 1) * 128], idbf[:, :]), s_mm, waits=[(s_cx, ldv), (s_v, s_v.n)])
                    P.op("vector", lambda e, xc=xc: e.tensor_copy(
                        xtok[:, xc * 128:(xc + 1) * 128], p_trb[:, :]),
                        waits=[(s_mm, s_mm.n)], inc=s_v)
                P.op("tensor", lambda e: e.matmul(
                    p_cb[:, :], sxc[:, 16 * 128:17 * 128], sxc[:, 17 * 128:18 * 128],
                    start=True, stop=True),
                    waits=[(s_cx, ldv)], inc=s_mm)
                P.op("vector", lambda e: e.tensor_mul(CBm[:, :], p_cb[:, :], tle_b[:, :]),
                     waits=[(s_mm, s_mm.n)], inc=s_v)
                import os as _os
                NHEADS_RUN = int(_os.environ.get("M2_HEADS", "32"))
                for h in range(NHEADS_RUN):
                    P.op("vector", lambda e, h=h: e.tensor_scalar(
                        dgh[:, :], cl_r[:, :], i32f[:, h:h + 1], None, ALU.mult),
                        inc=s_v)
                    P.op("tensor", lambda e: e.matmul(
                        p_x[:, :], o32f[:, :], dgh[:, :], start=True, stop=True),
                        waits=[(s_v, s_v.n), (s_ev, s_ev.n)], inc=s_mm)
                    xv = s_mm.n
                    P.op("vector", lambda e, h=h: e.tensor_scalar(
                        argb[:, :], p_x[:, :], biasT[:, h:h + 1], 20.0,
                        ALU.add, ALU.min),
                        waits=[(s_mm, xv)], inc=s_v)
                    P.op("scalar", lambda e: e.activation(Dp[:, :], argb[:, :], AF.Exp),
                         waits=[(s_v, s_v.n)], inc=s_ev)
                    P.op("scalar", lambda e: e.activation(expX[:, :], p_x[:, :], AF.Exp),
                         waits=[(s_mm, xv)], inc=s_ev)
                    ev_exp = s_ev.n
                    P.op("vector", lambda e: e.tensor_mul(Gm[:, :], Dp[:, :], CBm[:, :]),
                         waits=[(s_ev, ev_exp)], inc=s_v)
                    P.op("vector", lambda e: e.tensor_mul(
                        Ct[:, :], sxc[:, 17 * 128:18 * 128], expX[:, :]), inc=s_v)
                    P.op("vector", lambda e, h=h: e.tensor_scalar(
                        xsc[:, :], xtok[:, (h // 2) * 128 + (h % 2) * 64:
                                        (h // 2) * 128 + (h % 2) * 64 + 64],
                        wT[:, h:h + 1], None, ALU.mult), inc=s_v)
                    vv = s_v.n
                    half = (h % 2) * 64
                    P.op("tensor", lambda e, h=h, half=half: e.matmul(
                        p_y[half:half + 64, :],
                        xtok[:, (h // 2) * 128 + half:(h // 2) * 128 + half + 64],
                        Gm[:, :], start=True, stop=False),
                        waits=[(s_v, vv)], inc=s_mm)
                    P.op("tensor", lambda e, h=h, half=half: e.matmul(
                        p_y[half:half + 64, :], Hb[:, h * 64:(h + 1) * 64], Ct[:, :],
                        start=False, stop=True),
                        inc=s_mm)
                    P.op("tensor", lambda e, h=h: e.matmul(
                        p_dh[:, :], Btok[:, :], xsc[:, :], start=True, stop=True),
                        waits=[(s_v, vv)], inc=s_mm)
                    dhv = s_mm.n
                    P.op("vector", lambda e, h=h: e.scalar_tensor_tensor(
                        H_all[:, h * 64:(h + 1) * 64], H_all[:, h * 64:(h + 1) * 64],
                        gamb[:, h:h + 1], p_dh[:, :], ALU.mult, ALU.add),
                        waits=[(s_mm, dhv)], inc=s_v)
                    if h % 2 == 1:
                        xc = h // 2
                        P.op("vector", lambda e, xc=xc: e.tensor_scalar(
                            Gm[:, :],
                            sxc[:, xc * 128:(xc + 1) * 128], Drep[:, xc:xc + 1],
                            None, ALU.mult),
                            waits=[(s_mm, s_mm.n)], inc=s_v)
                        P.op("vector", lambda e, xc=xc: e.tensor_add(
                            yb[:, xc * 128:(xc + 1) * 128], p_y[:, :], Gm[:, :]),
                            inc=s_v)
                if hbm.get("dbgY") is not None:
                    _le = P.lastdone.get("vector")
                    if _le:
                        P.wait("sync", _le[0], _le[1])
                    _le2 = P.lastdone.get("scalar")
                    if _le2:
                        P.wait("sync", _le2[0], _le2[1])
                    P.op("sync", lambda e, ch=ch: e.dma_start(
                        out=hbm["dbgC"].ap()[0:32, ch * 128:(ch + 1) * 128], in_=cl_r[:, :]),
                        waits=[(s_st, s_st.n)], inc=s_st, inc_n=16)
                    P.op("sync", lambda e, ch=ch: e.dma_start(
                        out=hbm["dbgC"].ap()[32:64, ch * 128:(ch + 1) * 128], in_=wrm[:, :]),
                        waits=[(s_st, s_st.n)], inc=s_st, inc_n=16)
                    P.op("sync", lambda e, ch=ch: e.dma_start(
                        out=hbm["dbgC"].ap()[64:96, ch * 128: ch * 128 + 32],
                        in_=gamb[:, :].rearrange("p h -> p h")[0:32, :]),
                        waits=[(s_st, s_st.n)], inc=s_st, inc_n=16)
                    P.op("sync", lambda e, ch=ch: e.dma_start(
                        out=hbm["dbgC"].ap()[96:128, ch * 128: ch * 128 + 32],
                        in_=wT[0:32, :]),
                        waits=[(s_st, s_st.n)], inc=s_st, inc_n=16)
                    _le = P.lastdone.get("vector")
                    if _le:
                        P.wait("sync", _le[0], _le[1])
                    _le2 = P.lastdone.get("scalar")
                    if _le2:
                        P.wait("sync", _le2[0], _le2[1])
                    P.op("sync", lambda e, ch=ch: e.dma_start(
                        out=hbm["dbgY"].ap()[:, ch * 2048:(ch + 1) * 2048], in_=yb[:, :]),
                        waits=[(s_st, s_st.n)], inc=s_st, inc_n=16)
                    P.op("sync", lambda e, ch=ch: e.dma_start(
                        out=hbm["dbgH"].ap()[:, ch * 2048:(ch + 1) * 2048], in_=H_all[:, :]),
                        waits=[(s_st, s_st.n)], inc=s_st, inc_n=16)
                if hbm.get("dbgY") is not None:
                    P.wait("sync", s_st, s_st.n)
                    P.barrier()
                if not M2_TAIL:
                    continue
                P.op("vector", lambda e: e.tensor_mul(gbuf[:, :], yb[:, :], szc[:, :]),
                     inc=s_v)
                P.op("vector", lambda e: e.tensor_mul(sqb[:, :], gbuf[:, :], gbuf[:, :]),
                     inc=s_v)
                for ecc in range(16):
                    P.op("tensor", lambda e, ecc=ecc: e.matmul(
                        p_x[0:1, :], o128x1[:, :], sqb[:, ecc * 128:(ecc + 1) * 128],
                        start=(ecc == 0), stop=(ecc == 15)),
                        waits=[(s_v, s_v.n), (s_ev, s_ev.n)],
                        inc=(s_mm if ecc == 15 else None))
                P.op("scalar", lambda e: e.activation(
                    rstd1[:, :], p_x[0:1, :], AF.Ln, scale=np.float32(1.0 / 2048.0),
                    bias=np.float32(1e-6)),
                    waits=[(s_mm, s_mm.n)], inc=s_ev)
                P.op("scalar", lambda e: e.activation(
                    rstd1[:, :], rstd1[:, :], AF.Exp, scale=np.float32(-0.5)),
                    inc=s_ev)
                P.op("tensor", lambda e: e.matmul(
                    p_x[:, :], o1x128[:, :], rstd1[:, :], start=True, stop=True),
                    waits=[(s_ev, s_ev.n)], inc=s_mm)
                rbv = s_mm.n
                for ecc in range(16):
                    P.op("vector", lambda e, ecc=ecc, cc4=cc4: e.tensor_mul(
                        gnall[:, ecc * W + cc4 * 128: ecc * W + (cc4 + 1) * 128],
                        p_x[:, :], gbuf[:, ecc * 128:(ecc + 1) * 128]),
                        waits=[(s_mm, rbv)], inc=s_v)
            import os as _os4
            if _os4.environ.get("M2_TAIL", "1") != "1":
                P.barrier()
                continue
            for dc in range(8):
                pp = dc % 2
                for ecc in range(16):
                    P.op("tensor", lambda e, dc=dc, ecc=ecc, pp=pp: e.matmul(
                        p_o[pp][:, :],
                        opT[:, ecc * 1024 + dc * 128: ecc * 1024 + (dc + 1) * 128],
                        gnall[:, ecc * W:(ecc + 1) * W],
                        start=(ecc == 0), stop=(ecc == 15)),
                        waits=[(s_ev, s_ev.n), (s_v, s_v.n)],
                        inc=(s_mm if ecc == 15 else None))
                P.op("vector", lambda e, dc=dc, pp=pp: e.tensor_add(
                    ytile[:, :], p_o[pp][:, :], lnx2[dc][:, :]),
                    waits=[(s_mm, s_mm.n), (s_st, s_st.n)], inc=s_v)
                P.op("sync", lambda e, dc=dc, j=j: e.dma_start(
                    out=y1T[dc * 128:(dc + 1) * 128, j * W:(j + 1) * W], in_=ytile[:, :]),
                    waits=[(s_v, s_v.n), (s_st, s_st.n)], inc=s_st, inc_n=16)
            P.barrier()
        P.wait("sync", s_st, s_st.n)
        P.barrier()

    c12.close()

    with ExitStack() as c3:
        gwT = sbuf(c3, nc, "q_gwT", [128, 8 * 2752], BF16)
        uwT = sbuf(c3, nc, "q_uwT", [128, 8 * 2752], BF16)
        dwTh = sbuf(c3, nc, "q_dwTh", [128, 22 * 512], BF16)
        pw_s = sbuf(c3, nc, "q_pw", [128, 8], F32)
        sgu = sbuf(c3, nc, "q_sgu", [128, 22 * W], BF16)
        stmp = sbuf(c3, nc, "q_stmp", [128, W], BF16)
        xt3 = [sbuf(c3, nc, f"q_x3{c}", [128, W], F32) for c in range(8)]
        y1t = [sbuf(c3, nc, f"q_y1{c}", [128, W], F32) for c in range(8)]
        y1n = [sbuf(c3, nc, f"q_yn{c}", [128, W], BF16) for c in range(8)]
        sq3 = sbuf(c3, nc, "q_sq3", [128, W], F32)
        r3 = sbuf(c3, nc, "q_r3", [1, W], F32)
        p_g = [psum(c3, nc, f"q_pg{i}", [128, W]) for i in range(2)]
        p_u = psum(c3, nc, "q_pu", [128, W])
        p_s3 = psum(c3, nc, "q_ps3", [33, W])
        p_b3 = psum(c3, nc, "q_pb3", [128, W])

        sload(lambda e: e.dma_start(out=pw_s[:, :], in_=hbm["postw"].ap()))
        gload(lambda e: e.dma_start(
            out=gwT[:, :].rearrange("p (a d) -> p a d", a=8),
            in_=hbm["gate_wT"].ap().rearrange("(a p) d -> p a d", p=128)))
        gload(lambda e: e.dma_start(
            out=uwT[:, :].rearrange("p (a d) -> p a d", a=8),
            in_=hbm["up_wT"].ap().rearrange("(a p) d -> p a d", p=128)))
        wv = s_w.n
        lv3 = s_ld.n
        P.wait("vector", s_w, wv)
        P.wait("vector", s_ld, lv3)
        for c in range(8):
            P.op("vector", lambda e, c=c: e.tensor_scalar(
                gwT[:, c * 2752:(c + 1) * 2752], gwT[:, c * 2752:(c + 1) * 2752],
                pw_s[:, c:c + 1], None, ALU.mult), inc=s_v)
            P.op("vector", lambda e, c=c: e.tensor_scalar(
                uwT[:, c * 2752:(c + 1) * 2752], uwT[:, c * 2752:(c + 1) * 2752],
                pw_s[:, c:c + 1], None, ALU.mult), inc=s_v)
        P.barrier()

        for j in range(NT):
            for c in range(8):
                P.op("sync", lambda e, c=c, j=j: e.dma_start(
                    out=y1t[c][:, :], in_=y1T[c * 128:(c + 1) * 128, j * W:(j + 1) * W]),
                    waits=[(s_x, s_x.n)], inc=s_x, inc_n=16)
            y1v = s_x.n
            if hbm.get("dbgM3") is not None:
                P.op("sync", lambda e, j=j: e.dma_start(
                    out=hbm["dbgM3c"].ap()[:, j * W:(j + 1) * W], in_=y1t[0][:, :]),
                    waits=[(s_x, y1v), (s_st, s_st.n)], inc=s_st, inc_n=16)
            P.wait("scalar", s_x, y1v)
            P.barrier()
            sq0 = s_ev.n
            for c in range(8):
                P.op("scalar", lambda e, c=c: e.activation(sq3[:, :], y1t[c][:, :], AF.Square),
                     waits=[(s_mm, s_mm.n), (s_st, s_st.n)], inc=s_ev)
                P.op("tensor", lambda e, c=c: e.matmul(
                    p_s3[32:33, :], o128x1[:, :], sq3[:, :],
                    start=(c == 0), stop=(c == 7)),
                    waits=[(s_ev, s_ev.n)], inc=s_mm)
            if hbm.get("dbgM3") is not None:
                P.op("sync", lambda e, j=j: e.dma_start(
                    out=hbm["dbgM3f"].ap()[:, j * 130:j * 130 + 1], in_=o128x1[:, :],
                    allow_slow_non_contiguous=True),
                    waits=[(s_st, s_st.n)], inc=s_st, inc_n=16)
                P.op("sync", lambda e, j=j: e.dma_start(
                    out=hbm["dbgM3f"].ap()[0:1, j * 130 + 1:j * 130 + 129], in_=o1x128[:, :]),
                    waits=[(s_st, s_st.n)], inc=s_st, inc_n=16)
                P.wait("vector", s_mm, s_mm.n)
                P.op("vector", lambda e: e.tensor_copy(r3[:, :], p_s3[32:33, :]), inc=s_v)
                P.op("sync", lambda e, j=j: e.dma_start(
                    out=hbm["dbgM3e"].ap()[0:1, j * W:(j + 1) * W], in_=r3[:, :]),
                    waits=[(s_v, s_v.n), (s_st, s_st.n)], inc=s_st, inc_n=16)
                P.op("sync", lambda e, j=j: e.dma_start(
                    out=hbm["dbgM3d"].ap()[:, j * W:(j + 1) * W], in_=sq3[:, :]),
                    waits=[(s_ev, s_ev.n), (s_st, s_st.n)], inc=s_st, inc_n=16)
                P.wait("scalar", s_st, s_st.n)
            P.op("scalar", lambda e: e.activation(
                r3[:, :], p_s3[32:33, :], AF.Ln, scale=np.float32(1.0 / 1024.0),
                bias=np.float32(1e-6)),
                waits=[(s_mm, s_mm.n)], inc=s_ev)
            P.op("scalar", lambda e: e.activation(
                r3[:, :], r3[:, :], AF.Exp, scale=np.float32(-0.5)), inc=s_ev)
            P.op("tensor", lambda e: e.matmul(
                p_b3[:, :], o1x128[:, :], r3[:, :], start=True, stop=True),
                waits=[(s_ev, s_ev.n)], inc=s_mm)
            rbv = s_mm.n
            P.barrier()
            for c in range(8):
                P.op("vector", lambda e, c=c: e.tensor_mul(y1n[c][:, :], y1t[c][:, :], p_b3[:, :]),
                     waits=[(s_mm, rbv)], inc=s_v)
            if hbm.get("dbgM3") is not None:
                P.op("sync", lambda e, j=j: e.dma_start(
                    out=hbm["dbgM3"].ap()[0:1, j * W:(j + 1) * W], in_=r3[:, :]),
                    waits=[(s_mm, rbv), (s_st, s_st.n)], inc=s_st, inc_n=16)
                P.op("sync", lambda e, j=j: e.dma_start(
                    out=hbm["dbgM3b"].ap()[:, j * 1024:j * 1024 + 512], in_=y1n[0][:, :]),
                    waits=[(s_v, s_v.n), (s_st, s_st.n)], inc=s_st, inc_n=16)
            P.barrier()
            for ic in range(22):
                rows = 128 if ic < 21 else 64
                pp = ic % 2
                for c in range(8):
                    P.op("tensor", lambda e, ic=ic, c=c, pp=pp, rows=rows: e.matmul(
                        p_g[pp][0:rows, :],
                        gwT[:, c * 2752 + ic * 128: c * 2752 + ic * 128 + rows],
                        y1n[c][:, :], start=(c == 0), stop=(c == 7)),
                        waits=[(s_ev, s_ev.n - 1)], inc=(s_mm if c == 7 else None))
                for c in range(8):
                    P.op("tensor", lambda e, ic=ic, c=c, rows=rows: e.matmul(
                        p_u[0:rows, :],
                        uwT[:, c * 2752 + ic * 128: c * 2752 + ic * 128 + rows],
                        y1n[c][:, :], start=(c == 0), stop=(c == 7)),
                        waits=[(s_v, s_v.n)], inc=(s_mm if c == 7 else None))
                mmv = s_mm.n
                P.op("scalar", lambda e, pp=pp, rows=rows: e.activation(
                    stmp[0:rows, :], p_g[pp][0:rows, :], AF.Silu),
                    waits=[(s_mm, mmv)], inc=s_ev)
                P.op("vector", lambda e, ic=ic, rows=rows: e.tensor_mul(
                    sgu[0:rows, ic * W:(ic + 1) * W], p_u[0:rows, :], stmp[0:rows, :]),
                    waits=[(s_ev, s_ev.n)], inc=s_v)
            if hbm.get("dbgM3") is not None:
                P.op("sync", lambda e, j=j: e.dma_start(
                    out=hbm["dbgM3b"].ap()[:, j * 1024 + 512:(j + 1) * 1024],
                    in_=sgu[:, 0:512]),
                    waits=[(s_v, s_v.n), (s_st, s_st.n)], inc=s_st, inc_n=16)
            P.barrier()
            for half in range(2):
                for a0 in range(0, 21, 4):
                    na = min(4, 21 - a0)
                    P.op("gpsimd", lambda e, half=half, a0=a0, na=na: e.dma_start(
                        out=dwTh[:, a0 * 512:(a0 + na) * 512].rearrange(
                            "p (a d) -> p a d", a=na),
                        in_=hbm["down_wT"].ap()[a0 * 128:(a0 + na) * 128, :].rearrange(
                            "(a p) d -> p a d", p=128)[:, :, half * 512:(half + 1) * 512]),
                        waits=[(s_ld, s_ld.n), (s_mm, s_mm.n)], inc=s_ld, inc_n=16)
                hv = P.op("gpsimd", lambda e, half=half: e.dma_start(
                    out=dwTh[0:64, 21 * 512:22 * 512],
                    in_=hbm["down_wT"].ap()[2688:2752, half * 512:(half + 1) * 512]),
                    waits=[(s_ld, s_ld.n)], inc=s_ld, inc_n=16)
                P.wait("tensor", s_ld, hv)
                for dq in range(4):
                    dc = half * 4 + dq
                    pp = dq % 2
                    for ic in range(22):
                        rows = 128 if ic < 21 else 64
                        P.op("tensor", lambda e, dc=dc, dq=dq, ic=ic, pp=pp, rows=rows: e.matmul(
                            p_g[pp][:, :],
                            dwTh[0:rows, ic * 512 + dq * 128: ic * 512 + (dq + 1) * 128],
                            sgu[0:rows, ic * W:(ic + 1) * W],
                            start=(ic == 0), stop=(ic == 21)),
                            waits=[(s_v, s_v.n)], inc=(s_mm if ic == 21 else None))
                    v = P.op("sync", lambda e, dc=dc, j=j: e.dma_start(
                        out=xt3[dc][:, :], in_=xT[dc * 128:(dc + 1) * 128, j * W:(j + 1) * W]),
                        waits=[(s_x, s_x.n)], inc=s_x, inc_n=16)
                    P.op("vector", lambda e, dc=dc, pp=pp: e.tensor_add(
                        sq3[:, :], p_g[pp][:, :], y1t[dc][:, :]),
                        waits=[(s_mm, s_mm.n), (s_x, v), (s_st, s_st.n)], inc=s_v)
                    P.op("vector", lambda e, dc=dc: e.tensor_add(
                        sq3[:, :], sq3[:, :], xt3[dc][:, :]), inc=s_ev)
                    P.op("sync", lambda e, dc=dc, j=j: e.dma_start(
                        out=xoT[dc * 128:(dc + 1) * 128, j * W:(j + 1) * W], in_=sq3[:, :]),
                        waits=[(s_ev, s_ev.n), (s_st, s_st.n)], inc=s_st, inc_n=16)
            P.barrier()
        P.wait("sync", s_st, s_st.n)
        P.barrier()


def l1_attn_inputs(ins):
    m = {
        "in_projT": np.ascontiguousarray(ins["L1_attn_in_proj_w"].T),
        "norm_w": np.ascontiguousarray(ins["L1_attn_norm_w"].reshape(8, 128).T),
        "convw": np.ascontiguousarray(
            ins["L1_attn_conv_w"].reshape(18, 128, 4).transpose(1, 0, 2).reshape(128, 72)),
        "convb": np.ascontiguousarray(ins["L1_attn_conv_b"].reshape(18, 128).T),
        "dtb": ins["L1_attn_dt_bias"].reshape(32, 1),
        "A_log": ins["L1_attn_A_log"].reshape(32, 1),
        "Dvec": ins["L1_attn_D"].reshape(32, 1),
        "ssmw": np.ascontiguousarray(ins["L1_attn_ssm_norm_w"].reshape(16, 128).T),
        "out_projT": np.ascontiguousarray(ins["L1_attn_out_proj_w"].T),
        "postw": np.ascontiguousarray(ins["L1_attn_post_norm_w"].reshape(8, 128).T),
        "gate_wT": np.ascontiguousarray(ins["L1_attn_gate_w"].T),
        "up_wT": np.ascontiguousarray(ins["L1_attn_up_w"].T),
        "down_wT": np.ascontiguousarray(ins["L1_attn_down_w"].T),
    }
    return {k: np.ascontiguousarray(v.astype(np.float32)) for k, v in m.items()}


def build_l1_attn_nc():
    nc = bass.Bass("TRN2", target_bir_lowering=False, dynamic_dma_scratch_size=16384)
    hbm = {}
    hbm["x2T"] = nc.dram_tensor("x2T", [D, T_TOK], F32, kind="ExternalInput")
    hbm["x3T"] = nc.dram_tensor("x3T", [D, T_TOK], F32, kind="ExternalOutput")
    for nm, shp in (("in_projT", [D, 4384]), ("norm_w", [128, 8]),
                    ("convw", [128, 72]), ("convb", [128, 18]),
                    ("dtb", [32, 1]), ("A_log", [32, 1]), ("Dvec", [32, 1]),
                    ("ssmw", [128, 16]), ("out_projT", [2048, 1024]),
                    ("postw", [128, 8]), ("gate_wT", [D, 2752]),
                    ("up_wT", [D, 2752]), ("down_wT", [2752, 1024])):
        hbm[nm] = nc.dram_tensor(nm, shp, F32, kind="ExternalInput")
    import os
    dbg = os.environ.get("L1A_DBG", "0") == "1"
    kind = "ExternalOutput" if dbg else "Internal"
    for nm, shp, dt in (("szT", [2048, T_TOK], BF16), ("sxT", [2304, T_TOK], BF16),
                        ("y1T", [1024, T_TOK], F32)):
        hbm[nm] = nc.dram_tensor(nm, shp, dt, kind=kind)
    if dbg:
        hbm["dbgY"] = nc.dram_tensor("dbgY", [128, 32 * 2048], BF16, kind="ExternalOutput")
        hbm["dbgH"] = nc.dram_tensor("dbgH", [128, 32 * 2048], F32, kind="ExternalOutput")
        hbm["dbgC"] = nc.dram_tensor("dbgC", [128, 32 * 128], F32, kind="ExternalOutput")
        hbm["dbgD"] = nc.dram_tensor("dbgD", [96, T_TOK], F32, kind="ExternalOutput")
        hbm["dbgM3"] = nc.dram_tensor("dbgM3", [1, 8 * 512], F32, kind="ExternalOutput")
        hbm["dbgM3b"] = nc.dram_tensor("dbgM3b", [128, 8 * 1024], BF16, kind="ExternalOutput")
        hbm["dbgM3c"] = nc.dram_tensor("dbgM3c", [128, 8 * 512], F32, kind="ExternalOutput")
        hbm["dbgM3d"] = nc.dram_tensor("dbgM3d", [128, 8 * 512], F32, kind="ExternalOutput")
        hbm["dbgM3e"] = nc.dram_tensor("dbgM3e", [1, 8 * 512], F32, kind="ExternalOutput")
        hbm["dbgM3f"] = nc.dram_tensor("dbgM3f", [128, 8 * 130], F32, kind="ExternalOutput")
    else:
        hbm["dbgY"] = None
    with ExitStack() as ctx:
        C = make_consts(nc)
        P = Prog(nc, ctx)
        emit_l1_attn(nc, ctx, P, C, hbm)
        P.run()
    return nc



def emit_l1_ffn(nc, ctx, P, C, hbm, xin="x3T", xout="x4T"):
    W = 512
    NT = T_TOK // W
    xT = hbm[xin].ap()
    xoT = hbm[xout].ap()

    wmb = sbuf(ctx, nc, "m_wm", [128, 8 * 1024], BF16)
    w1e = sbuf(ctx, nc, "m_w1", [128, 16 * 512], BF16)
    w2e = sbuf(ctx, nc, "m_w2", [128, 64 * 128], BF16)
    wob = sbuf(ctx, nc, "m_wo", [128, 8 * 1024], BF16)
    embb = sbuf(ctx, nc, "m_emb", [128, 16], BF16)
    bm_s = sbuf(ctx, nc, "m_bm", [128, 8], F32)
    bo_s = sbuf(ctx, nc, "m_bo", [128, 8], F32)
    b1c = sbuf(ctx, nc, "m_b1c", [128, 64], F32)
    b2s = sbuf(ctx, nc, "m_b2", [16, 128], F32)
    hT = sbuf(ctx, nc, "m_hT", [128, 8 * W], BF16)
    gtp = [sbuf(ctx, nc, f"m_gt{i}", [128, 4 * W], BF16) for i in range(2)]
    comb2 = sbuf(ctx, nc, "m_comb", [128, 8 * W], F32)
    combB = sbuf(ctx, nc, "m_combB", [128, 8 * W], BF16)
    wta = sbuf(ctx, nc, "m_wta", [128, 32 * 16], F32)
    wtrT = sbuf(ctx, nc, "m_wtrT", [16, 32 * 128], F32)
    wtrB = sbuf(ctx, nc, "m_wtrB", [16, 32 * 128], BF16)
    vtmp = sbuf(ctx, nc, "m_vtmp", [128, W], F32)
    wrepS = sbuf(ctx, nc, "m_wrepS", [128, W], F32)
    p16 = sbuf(ctx, nc, "m_p16", [128, 16], F32)
    r1 = sbuf(ctx, nc, "m_r1", [128, 1], F32)
    rz = sbuf(ctx, nc, "m_rz", [128, 1], F32)
    zt = sbuf(ctx, nc, "m_zt", [128, 16], F32)
    idf32 = sbuf(ctx, nc, "m_idf", [128, 128], F32)
    o1x128 = sbuf(ctx, nc, "m_o1x", [1, 128], F32)
    selb = sbuf(ctx, nc, "m_selb", [16, 16 * 128], BF16)
    o128x1 = sbuf(ctx, nc, "m_o128", [128, 1], F32)
    xt = [sbuf(ctx, nc, f"m_x{c}", [128, W], F32) for c in range(8)]
    lnx = [sbuf(ctx, nc, f"m_ln{c}", [128, W], BF16) for c in range(8)]

    p_h = [psum(ctx, nc, f"m_ph{i}", [128, W]) for i in range(2)]
    p_mid = psum(ctx, nc, "m_pmid", [128, 4 * W])
    p_eo = [psum(ctx, nc, f"m_peo{i}", [128, W]) for i in range(2)]
    p_sm = p_mid[:, 1536:1664]

    s_w = Sem(ctx, nc, "n_sw")
    s_x = Sem(ctx, nc, "n_sx")
    s_mm = Sem(ctx, nc, "n_smm")
    s_ev = Sem(ctx, nc, "n_sev")
    s_y = Sem(ctx, nc, "n_sy")
    s_st = Sem(ctx, nc, "n_sst")
    s_rt = Sem(ctx, nc, "n_srt")
    s_cb = Sem(ctx, nc, "n_scb")

    ln = LNKit(ctx, nc, P, C, W=W, tag="n_lnk", ones_col=o128x1, ones_row=o1x128,
               psums=(p_mid[0:33, 0:512], p_mid[:, 512:1024], p_mid[:, 1024:1536]))

    def gload(fn):
        P.op("gpsimd", fn, waits=[(s_w, s_w.n)], inc=s_w, inc_n=16)

    gload(lambda e: e.dma_start(
        out=wmb[:, :].rearrange("p (a d) -> p a d", a=8),
        in_=hbm["Wm"].ap().rearrange("(a p) d -> p a d", p=128)))
    gload(lambda e: e.dma_start(
        out=wob[:, :].rearrange("p (a d) -> p a d", a=8),
        in_=hbm["Wo2"].ap().rearrange("(a p) d -> p a d", p=128)))
    gload(lambda e: e.dma_start(
        out=w1e[:, :].rearrange("p (a d) -> p a d", a=16),
        in_=hbm["W1"].ap().rearrange("(a p) d -> p a d", p=128)))
    gload(lambda e: e.dma_start(out=w2e[:, :], in_=hbm["W2r"].ap()))
    gload(lambda e: e.dma_start(out=embb[:, :], in_=hbm["emb"].ap()))
    for nm, dst in (("bm2", bm_s), ("bo2", bo_s), ("b1c", b1c), ("b2n", b2s)):
        gload(lambda e, nm=nm, dst=dst: e.dma_start(out=dst[:, :], in_=hbm[nm].ap()))
    gload(lambda e: e.dma_start(out=idf32[:, :], in_=C["identity"].ap()[:, :]))
    gload(lambda e: e.dma_start(out=o1x128[:, :], in_=C["ones1x128"].ap()[:, :]))
    gload(lambda e: e.dma_start(out=o128x1[:, :], in_=C["ones128x1"].ap()[:, :]))
    gload(lambda e: e.dma_start(out=selb[:, :], in_=C["selb16"].ap()[:, :]))
    wv = s_w.n
    P.wait("tensor", s_w, wv)
    P.wait("vector", s_w, wv)
    P.wait("scalar", s_w, wv)
    P.barrier()

    for j in range(NT):
        for c in range(8):
            P.op("sync", lambda e, c=c, j=j: e.dma_start(
                out=xt[c][:, :], in_=xT[c * 128:(c + 1) * 128, j * W:(j + 1) * W]),
                waits=[(s_st, s_st.n)], inc=s_x, inc_n=16)
        xw = [(s_x, s_x.n)] * 8
        P.wait("vector", s_st, s_st.n)
        ln.emit([xt[c][:, :] for c in range(8)],
                [lnx[c][:, :] for c in range(8)], xw)
        P.barrier()
        for dc in range(8):
            pp = dc % 2
            for c in range(8):
                P.op("tensor", lambda e, dc=dc, c=c, pp=pp: e.matmul(
                    p_h[pp][:, :],
                    wmb[:, c * 1024 + dc * 128: c * 1024 + (dc + 1) * 128],
                    lnx[c][:, :], start=(c == 0), stop=(c == 7)),
                    waits=[(s_ev, s_ev.n - 1)], inc=(s_mm if c == 7 else None))
            P.op("vector", lambda e, dc=dc, pp=pp: e.tensor_scalar(
                hT[:, dc * W:(dc + 1) * W], p_h[pp][:, :], bm_s[:, dc:dc + 1],
                None, ALU.add),
                waits=[(s_mm, s_mm.n)], inc=s_ev)
        P.barrier()
        for h in range(8):
            for ts in range(4):
                P.op("tensor", lambda e, h=h, ts=ts: e.matmul(
                    p_sm[:, 0:16],
                    hT[:, h * W + ts * 128: h * W + (ts + 1) * 128],
                    embb[:, :], start=True, stop=True),
                    waits=[(s_rt, s_rt.n)], inc=s_mm)
                mmv = s_mm.n
                P.op("vector", lambda e: e.reduce_max(r1[:, :], p_sm[:, 0:16], AX.X),
                     waits=[(s_mm, mmv)])
                P.op("vector", lambda e: e.tensor_scalar(r1[:, :], r1[:, :], -1.0,
                                                         None, ALU.mult), inc=s_rt)
                P.op("scalar", lambda e: e.activation(
                    p16[:, :], p_sm[:, 0:16], AF.Exp, bias=r1[:, 0:1],
                    accum_out=rz[:, 0:1]), waits=[(s_rt, s_rt.n)], inc=s_rt)
                P.op("vector", lambda e: e.reciprocal(rz[:, :], rz[:, :]),
                     waits=[(s_rt, s_rt.n)])
                P.op("vector", lambda e: e.reduce_max(r1[:, :], p16[:, :], AX.X))
                P.op("vector", lambda e: e.tensor_scalar(zt[:, :], p16[:, :], r1[:, 0:1],
                                                         None, ALU.is_equal))
                P.op("vector", lambda e: e.tensor_mul(zt[:, :], zt[:, :], p16[:, :]))
                P.op("vector", lambda e: e.tensor_sub(zt[:, :], p16[:, :], zt[:, :]))
                P.op("vector", lambda e: e.reduce_max(r1[:, :], zt[:, :], AX.X))
                P.op("vector", lambda e: e.tensor_scalar(zt[:, :], p16[:, :], r1[:, 0:1],
                                                         None, ALU.is_ge))
                P.op("vector", lambda e: e.tensor_mul(zt[:, :], zt[:, :], p16[:, :]))
                P.op("vector", lambda e, h=h, ts=ts: e.tensor_scalar(
                    wta[:, (h * 4 + ts) * 16:(h * 4 + ts) * 16 + 16],
                    zt[:, :], rz[:, 0:1], None, ALU.mult), inc=s_rt)
                P.transpose_l(lambda e, h=h, ts=ts: (p_sm[0:16, 0:128],
                    wta[:, (h * 4 + ts) * 16:(h * 4 + ts) * 16 + 16], idf32[:, :]), s_mm, waits=[(s_rt, s_rt.n)])
                P.op("vector", lambda e, h=h, ts=ts: e.tensor_copy(
                    wtrT[:, (h * 4 + ts) * 128:(h * 4 + ts + 1) * 128], p_sm[0:16, 0:128]),
                    waits=[(s_mm, s_mm.n)], inc=s_rt)
                P.op("vector", lambda e, h=h, ts=ts: e.tensor_copy(
                    wtrB[:, (h * 4 + ts) * 128:(h * 4 + ts + 1) * 128], p_sm[0:16, 0:128]),
                    inc=s_rt)
        for h in range(8):
            P.op("tensor", lambda e, h=h: e.matmul(
                p_h[1][:, :], b2s[:, :], wtrT[0:16, h * W:(h + 1) * W],
                start=True, stop=True),
                waits=[(s_rt, s_rt.n), (s_cb, s_cb.n)], inc=s_mm)
            P.op("vector", lambda e, h=h: e.tensor_copy(
                comb2[:, h * W:(h + 1) * W], p_h[1][:, :]),
                waits=[(s_mm, s_mm.n)], inc=s_cb)
        P.barrier()
        down_done = [s_mm.n, s_mm.n]
        vread_done = [s_cb.n, s_cb.n]
        wrep_read = s_cb.n
        gelu_half = [s_ev.n, s_ev.n]
        for ee in range(16):
            for h in range(8):
                it = ee * 8 + h
                gg = it % 2
                pp = it % 2
                mmh = [0, 0]
                for half in range(2):
                    hb = (it * 2 + half) % 2
                    for k in range(2):
                        ic = half * 2 + k
                        P.op("tensor", lambda e, ee=ee, h=h, ic=ic, hb=hb, k=k: e.matmul(
                            p_mid[:, (hb * 2 + k) * W:(hb * 2 + k + 1) * W],
                            w1e[:, ee * 512 + ic * 128: ee * 512 + (ic + 1) * 128],
                            hT[:, h * W:(h + 1) * W], start=True, stop=True),
                            waits=[(s_ev, gelu_half[hb])],
                            inc=(s_mm if k == 1 else None))
                    mmh[half] = s_mm.n
                for half in range(2):
                    hb = (it * 2 + half) % 2
                    for k in range(2):
                        ic = half * 2 + k
                        P.op("scalar", lambda e, ee=ee, ic=ic, gg=gg, hb=hb, k=k: e.activation(
                            gtp[gg][:, ic * W:(ic + 1) * W],
                            p_mid[:, (hb * 2 + k) * W:(hb * 2 + k + 1) * W],
                            AF.Gelu, bias=b1c[:, ic * 16 + ee:ic * 16 + ee + 1]),
                            waits=[(s_mm, mmh[half]), (s_mm, down_done[gg])], inc=s_ev)
                    gelu_half[hb] = s_ev.n
                gv = s_ev.n
                wrv = P.op("tensor", lambda e, ee=ee, h=h: e.matmul(
                    p_h[0][:, :], selb[:, ee * 128:(ee + 1) * 128],
                    wtrB[0:16, h * W:(h + 1) * W],
                    start=True, stop=True),
                    waits=[(s_cb, wrep_read)], inc=s_mm)
                P.op("vector", lambda e: e.tensor_copy(wrepS[:, :], p_h[0][:, :]),
                     waits=[(s_mm, wrv)], inc=s_cb)
                wrep_read = s_cb.n
                for ic in range(4):
                    P.op("tensor", lambda e, ee=ee, ic=ic, gg=gg, pp=pp: e.matmul(
                        p_eo[pp][:, :],
                        w2e[:, (ee * 4 + ic) * 128:(ee * 4 + ic + 1) * 128],
                        gtp[gg][:, ic * W:(ic + 1) * W],
                        start=(ic == 0), stop=(ic == 3)),
                        waits=[(s_ev, gv), (s_cb, vread_done[pp])],
                        inc=(s_mm if ic == 3 else None))
                down_done[gg] = s_mm.n
                P.op("vector", lambda e, pp=pp: e.tensor_mul(
                    vtmp[:, :], p_eo[pp][:, :], wrepS[:, :]),
                    waits=[(s_mm, s_mm.n)], inc=s_cb)
                vread_done[pp] = s_cb.n
                P.op("vector", lambda e, h=h: e.tensor_add(
                    comb2[:, h * W:(h + 1) * W], comb2[:, h * W:(h + 1) * W],
                    vtmp[:, :]), inc=s_cb)
        P.barrier()
        for c in range(8):
            P.op("vector", lambda e, c=c: e.tensor_copy(
                combB[:, c * W:(c + 1) * W], comb2[:, c * W:(c + 1) * W]), inc=s_cb)
        for dc in range(8):
            pp = dc % 2
            for c in range(8):
                P.op("tensor", lambda e, dc=dc, c=c, pp=pp: e.matmul(
                    p_h[pp][:, :],
                    wob[:, c * 1024 + dc * 128: c * 1024 + (dc + 1) * 128],
                    combB[:, c * W:(c + 1) * W], start=(c == 0), stop=(c == 7)),
                    waits=[(s_cb, s_cb.n), (s_ev, s_ev.n - 1)], inc=(s_mm if c == 7 else None))
            st_prev = s_st.n
            P.op("scalar", lambda e, dc=dc, pp=pp: e.activation(
                ln.tmp[:, :], p_h[pp][:, :], AF.Identity, bias=bo_s[:, dc:dc + 1]),
                waits=[(s_mm, s_mm.n), (s_st, st_prev)], inc=s_ev)
            P.op("vector", lambda e, dc=dc: e.tensor_add(ln.tmp[:, :], ln.tmp[:, :], xt[dc][:, :]),
                 waits=[(s_ev, s_ev.n)], inc=s_y)
            P.op("sync", lambda e, dc=dc, j=j: e.dma_start(
                out=xoT[dc * 128:(dc + 1) * 128, j * W:(j + 1) * W], in_=ln.tmp[:, :]),
                waits=[(s_y, s_y.n)], inc=s_st, inc_n=16)
        P.barrier()
    P.wait("sync", s_st, s_st.n)
    P.barrier()


def l1_ffn_inputs(ins):
    W2r = np.ascontiguousarray(
        ins["L1_ffn_W2"].reshape(16, 4, 128, 128).transpose(2, 0, 1, 3).reshape(128, 16 * 4 * 128))
    b1c = np.ascontiguousarray(
        ins["L1_ffn_b1"].T.reshape(4, 128, 16).transpose(1, 0, 2).reshape(128, 64))
    m = {
        "Wm": ins["L1_ffn_Wm"],
        "Wo2": ins["L1_ffn_Wo"],
        "W1": ins["L1_ffn_W1"].reshape(16 * 128, 512),
        "W2r": W2r,
        "emb": ins["L1_ffn_expert_emb"],
        "bm2": np.ascontiguousarray(ins["L1_ffn_bm"].reshape(8, 128).T),
        "bo2": np.ascontiguousarray(ins["L1_ffn_bo"].reshape(8, 128).T),
        "b1c": b1c,
        "b2n": ins["L1_ffn_b2"],
    }
    return {k: np.ascontiguousarray(v.astype(np.float32)) for k, v in m.items()}


def build_l1_ffn_nc():
    nc = bass.Bass("TRN2", target_bir_lowering=False, dynamic_dma_scratch_size=16384)
    hbm = {}
    hbm["x3T"] = nc.dram_tensor("x3T", [D, T_TOK], F32, kind="ExternalInput")
    hbm["x4T"] = nc.dram_tensor("x4T", [D, T_TOK], F32, kind="ExternalOutput")
    for nm, shp in (("Wm", [D, D]), ("Wo2", [D, D]), ("W1", [2048, 512]),
                    ("W2r", [128, 8192]), ("emb", [128, 16]),
                    ("bm2", [128, 8]), ("bo2", [128, 8]),
                    ("b1c", [128, 64]), ("b2n", [16, 128])):
        hbm[nm] = nc.dram_tensor(nm, shp, F32, kind="ExternalInput")
    with ExitStack() as ctx:
        C = make_consts(nc)
        P = Prog(nc, ctx)
        emit_l1_ffn(nc, ctx, P, C, hbm)
        P.run()
    return nc



def build_l0_attn_nc():
    nc = bass.Bass("TRN2", target_bir_lowering=False, dynamic_dma_scratch_size=16384)
    hbm = {}
    hbm["xT"] = nc.dram_tensor("xT", [D, T_TOK], F32, kind="ExternalInput")
    hbm["x1T"] = nc.dram_tensor("x1T", [D, T_TOK], F32, kind="ExternalOutput")
    for nm, shp in (("Wq", [D, D]), ("Wk", [D, D]), ("Wv", [D, D]), ("Wo", [D, D])):
        hbm[nm] = nc.dram_tensor(nm, shp, F32, kind="ExternalInput")
    for nm, shp in (("bq", [128, 8]), ("bk", [128, 8]), ("bo", [128, 8]),
                    ("bv", [1, D])):
        hbm[nm] = nc.dram_tensor(nm, shp, F32, kind="ExternalInput")
    with ExitStack() as ctx:
        C = make_consts(nc)
        P = Prog(nc, ctx)
        emit_l0_attn(nc, ctx, P, C, hbm)
        P.run()
    return nc


def l0_attn_inputs(x_core, ins):
    m = {
        "xT": np.ascontiguousarray(x_core.T),
        "Wq": ins["L0_attn_Wq"], "Wk": ins["L0_attn_Wk"],
        "Wv": ins["L0_attn_Wv"], "Wo": ins["L0_attn_Wo"],
        "bq": np.ascontiguousarray(ins["L0_attn_bq"].reshape(8, 128).T),
        "bk": np.ascontiguousarray(ins["L0_attn_bk"].reshape(8, 128).T),
        "bo": np.ascontiguousarray(ins["L0_attn_bo"].reshape(8, 128).T),
        "bv": ins["L0_attn_bv"].reshape(1, D),
    }
    return {k: np.ascontiguousarray(v.astype(np.float32)) for k, v in m.items()}


def build_fused_nc():
    nc = bass.Bass("TRN2", target_bir_lowering=False, dynamic_dma_scratch_size=16384)
    hbm = {}
    hbm["xT"] = nc.dram_tensor("xT", [D, T_TOK], F32, kind="ExternalInput")
    for nm in ("x1T", "x2T", "x3T"):
        hbm[nm] = nc.dram_tensor(nm, [D, T_TOK], F32, kind="Internal")
    hbm["x4T"] = nc.dram_tensor("x4T", [D, T_TOK], F32, kind="ExternalOutput")
    for nm, shp in (("Wq", [D, D]), ("Wk", [D, D]), ("Wv", [D, D]), ("Wo", [D, D]),
                    ("bq", [128, 8]), ("bk", [128, 8]), ("bo", [128, 8]),
                    ("bv", [1, D]),
                    ("W_exp", [D, 4096]), ("W_conT", [D, 4096]),
                    ("uw", [12, 4, 2048]), ("dw", [12, 4, 2048]),
                    ("b_exp", [1, 4096]), ("up_bias", [1, 4096]),
                    ("b_con2", [128, 8]), ("down_bias2", [128, 8]),
                    ("in_projT", [D, 4384]), ("norm_w", [128, 8]),
                    ("convw", [128, 72]), ("convb", [128, 18]),
                    ("dtb", [32, 1]), ("A_log", [32, 1]), ("Dvec", [32, 1]),
                    ("ssmw", [128, 16]), ("out_projT", [2048, 1024]),
                    ("postw", [128, 8]), ("gate_wT", [D, 2752]),
                    ("up_wT", [D, 2752]), ("down_wT", [2752, 1024]),
                    ("Wm", [D, D]), ("Wo2", [D, D]), ("W1", [2048, 512]),
                    ("W2r", [128, 8192]), ("emb", [128, 16]),
                    ("bm2", [128, 8]), ("bo2", [128, 8]),
                    ("b1c", [128, 64]), ("b2n", [16, 128])):
        hbm[nm] = nc.dram_tensor(nm, shp, F32, kind="ExternalInput")
    for nm, shp, dt in (("szT", [2048, T_TOK], BF16), ("sxT", [2304, T_TOK], BF16),
                        ("y1T", [1024, T_TOK], F32)):
        hbm[nm] = nc.dram_tensor(nm, shp, dt, kind="Internal")
    hbm["dbgY"] = None
    with ExitStack() as ctx:
        C = make_consts(nc)
        P = Prog(nc, ctx)
        P.tag = "L0a"
        with ExitStack() as c:
            emit_l0_attn(nc, c, P, C, hbm)
            P.barrier()
        P.tag = "L0f"
        with ExitStack() as c:
            emit_l0_ffn(nc, c, P, C, hbm)
            P.barrier()
        P.tag = "L1a"
        with ExitStack() as c:
            emit_l1_attn(nc, c, P, C, hbm)
            P.barrier()
        P.tag = "L1f"
        with ExitStack() as c:
            emit_l1_ffn(nc, c, P, C, hbm)
            P.barrier()
        P.run()
    return nc


def kernel(**inputs):
    x = np.asarray(inputs["L0_attn_x"], np.float32)
    nc = build_fused_nc()
    base = {}
    base.update(l0_ffn_inputs(inputs))
    base.update(l1_attn_inputs(inputs))
    base.update(l1_ffn_inputs(inputs))
    base.update({k: v for k, v in l0_attn_inputs(x[0], inputs).items() if k != "xT"})
    maps = [dict(base, xT=np.ascontiguousarray(x[c].T)) for c in range(8)]
    res = run_bass_kernel_spmd(nc, maps, core_ids=list(range(8)))
    out = np.stack([np.asarray(res.results[c]["x4T"]).T for c in range(8)], axis=0)
    return out.astype(np.float32)


def kernel_multilaunch(**inputs):
    x = np.asarray(inputs["L0_attn_x"], np.float32)

    nc1 = build_l0_attn_nc()
    maps1 = []
    for c in range(8):
        m = l0_attn_inputs(x[c], inputs)
        maps1.append(m)
    res1 = run_bass_kernel_spmd(nc1, maps1, core_ids=list(range(8)))
    x1 = [np.asarray(res1.results[c]["x1T"]) for c in range(8)]

    nc2 = build_l0_ffn_nc()
    base2 = l0_ffn_inputs(inputs)
    maps2 = [dict(base2, x1T=x1[c]) for c in range(8)]
    res2 = run_bass_kernel_spmd(nc2, maps2, core_ids=list(range(8)))
    x2 = [np.asarray(res2.results[c]["x2T"]) for c in range(8)]

    nc3 = build_l1_attn_nc()
    base3 = l1_attn_inputs(inputs)
    maps3 = [dict(base3, x2T=x2[c]) for c in range(8)]
    res3 = run_bass_kernel_spmd(nc3, maps3, core_ids=list(range(8)))
    x3 = [np.asarray(res3.results[c]["x3T"]) for c in range(8)]

    nc4 = build_l1_ffn_nc()
    base4 = l1_ffn_inputs(inputs)
    maps4 = [dict(base4, x3T=x3[c]) for c in range(8)]
    res4 = run_bass_kernel_spmd(nc4, maps4, core_ids=list(range(8)))
    out = np.stack([np.asarray(res4.results[c]["x4T"]).T for c in range(8)], axis=0)
    return out.astype(np.float32)
